# version 05
# baseline (speedup 1.0000x reference)
import contextlib
import math

import numpy as np

import concourse.bass as bass
import concourse.mybir as mybir
from concourse.bass_utils import run_bass_kernel_spmd

F32 = mybir.dt.float32
BF16 = mybir.dt.bfloat16
AF = mybir.ActivationFunctionType
ALU = mybir.AluOpType
AX = mybir.AxisListType

P4 = 4096
NPH = 16
L = 2048
D = 1024
DH = 512
NB = 2
NCORES = 8


def _red(m):
    r = np.mod(m, 2 * P4)
    r = np.where(r > P4, r - 2 * P4, r)
    return r.astype(np.float32)


def GGBQ(self, q, c):
    base = (q % 2) * 4096 + c * 2048
    return self.GGB[:, base:base + 2048]


def host_tables():
    COS = P4 // 2
    r = np.arange(128)
    kap_of_r = 8 * (r % 16) + r // 16
    b = np.arange(128)
    base1 = 16 * np.outer(b, 2 * kap_of_r + 1)
    t_l1 = np.stack([_red(base1 + COS), _red(-base1)])
    m = np.arange(128)
    basei = 16 * np.outer(2 * kap_of_r + 1, m)
    t_il1 = np.stack([_red(basei + COS), _red(-basei)])
    a = np.arange(NPH)
    t_l2 = np.zeros((3, 16, 128, 128), np.float32)
    t_il2 = np.zeros((3, 16, 128, 128), np.float32)
    for g in range(16):
        for gam in range(8):
            kpp = 8 * g + gam
            rows = a * 8 + gam
            th = a[:, None] * (2 * kpp + 1) + 512 * np.outer(a, a)
            t_l2[0, g, rows[:, None], rows[None, :]] = _red(th + COS)
            t_l2[1, g, rows[:, None], rows[None, :]] = _red(th)
            t_l2[2, g, rows[:, None], rows[None, :]] = _red(-th)
            thi = a[None, :] * (2 * kpp + 1) + 512 * np.outer(a, a)
            t_il2[0, g, rows[:, None], rows[None, :]] = _red(thi + COS)
            t_il2[1, g, rows[:, None], rows[None, :]] = _red(thi)
            t_il2[2, g, rows[:, None], rows[None, :]] = _red(-thi)
    return t_l1, t_l2, t_il2, t_il1


class Builder:
    def __init__(self, debug_dumps=(), stop_after="full", batch_stop=None, ffn_stop=None):
        self.nc = bass.Bass("TRN2", target_bir_lowering=False)
        self.debug_dumps = set(debug_dumps)
        self.stop_after = stop_after
        self.batch_stop = batch_stop
        self.ffn_stop = ffn_stop
        self.dbg = {}

    def sem_inc(self, inst, name, k=1):
        inst.then_inc(self.sem[name], k)
        self.cnt[name] += k
        return self.cnt[name]

    def fence(self, e, inst, name):
        self.sem_inc(inst, name)
        e.wait_ge(self.sem[name], self.cnt[name])

    def maybe_dump(self, tag, src_ap, shape, dtype):
        if tag not in self.debug_dumps:
            return
        t = self.nc.dram_tensor("dbg_" + tag, shape, dtype, kind="ExternalOutput").ap()
        self.dbg[tag] = t
        with self.nc.Block() as blk:
            @blk.sync
            def _(e):
                self.sem_inc(e.dma_start(t, src_ap), "st", 16)
                e.wait_ge(self.sem["st"], self.cnt["st"])

    def build(self):
        nc = self.nc
        self.x_in = nc.dram_tensor("x", [NB, L, D], F32, kind="ExternalInput").ap()
        self.pw_in = nc.dram_tensor("pw", [D, 3 * D], F32, kind="ExternalInput").ap()
        self.pb_in = nc.dram_tensor("pb", [3 * D], F32, kind="ExternalInput").ap()
        self.ft_in = nc.dram_tensor("ft", [2, L, D], F32, kind="ExternalInput").ap()
        self.w1_in = nc.dram_tensor("w1", [D, 4 * D], F32, kind="ExternalInput").ap()
        self.b1_in = nc.dram_tensor("b1", [4 * D], F32, kind="ExternalInput").ap()
        self.w2_in = nc.dram_tensor("w2", [4 * D, D], F32, kind="ExternalInput").ap()
        self.b2_in = nc.dram_tensor("b2", [D], F32, kind="ExternalInput").ap()
        self.tl1_in = nc.dram_tensor("tl1", [2, 128, 128], F32, kind="ExternalInput").ap()
        self.tl2_in = nc.dram_tensor("tl2", [3, 16, 128, 128], F32, kind="ExternalInput").ap()
        self.til2_in = nc.dram_tensor("til2", [3, 16, 128, 128], F32, kind="ExternalInput").ap()
        self.til1_in = nc.dram_tensor("til1", [2, 128, 128], F32, kind="ExternalInput").ap()
        self.id_in = nc.dram_tensor("idm", [128, 128], F32, kind="ExternalInput").ap()
        self.out = nc.dram_tensor("out", [NB, L, D], F32, kind="ExternalOutput").ap()

        self.g1sp = nc.dram_tensor("g1sp", [NB, 16, 128, D], BF16).ap()
        self.g2sp = nc.dram_tensor("g2sp", [NB, 16, 128, D], BF16).ap()
        self.x1sp = nc.dram_tensor("x1sp", [NB, 16, 128, D], F32).ap()
        self.gts = nc.dram_tensor("gts", [2, 2, 2, 128, 16 * DH], BF16).ap()
        self.hs1_t = nc.dram_tensor("hs1", [2, 16, 128, DH], BF16)
        self.hs2_t = nc.dram_tensor("hs2", [2, 128, 16, DH], BF16)
        self.hs1 = self.hs1_t.ap()

        es = contextlib.ExitStack()
        with es:
            sem_names = ["ld", "gl", "st", "pe", "ev", "sc", "te", "sA", "sB", "tA", "tB", "rwv", "rws", "gp", "rwg"]
            self.sem = {n: es.enter_context(nc.semaphore(n)) for n in sem_names}
            self.cnt = {n: 0 for n in sem_names}
            sb = lambda name, shape, dt: es.enter_context(nc.sbuf_tensor(name, shape, dt))

            self.IDENT = sb("IDENT", [128, 128], BF16)
            self.EPS = sb("EPS", [128, 1], F32)
            self.STT = sb("STT", [128, 16], F32)
            self.XT = sb("XT", [128, 2 * D], F32)
            self.SQ = sb("SQ", [128, D], F32)
            self.LNT = sb("LNT", [128, 2 * D], BF16)
            self.pmm = [es.enter_context(nc.psum_tensor(f"pm{i}", [128, 512], F32))
                        for i in range(6)]
            self.ptr = es.enter_context(nc.psum_tensor("ptr", [128, 1024], BF16))

            with nc.Block() as blk:
                @blk.vector
                def _(e):
                    e.memset(self.EPS[:, :], 1e-5)

                @blk.gpsimd
                def _(e):
                    self.sem_inc(e.dma_start(self.IDENT[:, :], self.id_in[:, :]), "gl", 16)
            self.gl_ident = self.cnt["gl"]

            with contextlib.ExitStack() as tes:
                tsb = lambda name, shape, dt: tes.enter_context(nc.sbuf_tensor(name, shape, dt))
                self.TB_L1 = tsb("TB_L1", [128, 256], BF16)
                self.TB_IL1 = tsb("TB_IL1", [128, 256], BF16)
                self.TB_L2 = tsb("TB_L2", [128, 48 * 128], BF16)
                self.TB_IL2 = tsb("TB_IL2", [128, 48 * 128], BF16)
                self.phase_tables(tsb)
                if self.stop_after != "tables":
                    if self.stop_after != "notables_filter":
                        self.phase_filter_prep()
                    if self.stop_after not in ("filter",):
                        for b in range(NB):
                            self.phase_batch(b, es)
                            if self.stop_after == "b0":
                                break
            if self.stop_after == "full":
                self.phase_ffn(es)

            cnt = dict(self.cnt)
            with nc.Block() as blk:
                @blk.sync
                def _(e):
                    for n in ("ld", "st", "sA", "sB", "tA", "tB"):
                        if cnt[n]:
                            e.wait_ge(self.sem[n], cnt[n])

                @blk.gpsimd
                def _(e):
                    e.wait_ge(self.sem["gl"], cnt["gl"])
        return nc

    def phase_tables(self, tsb):
        nc = self.nc
        stg_es = contextlib.ExitStack()
        ssb = lambda name, shape, dt: stg_es.enter_context(nc.sbuf_tensor(name, shape, dt))
        TSG = ssb("TSG", [128, 48 * 128], F32)
        TSG2 = ssb("TSG2", [128, 512], F32)
        TMPF = ssb("TMPF", [128, 256], F32)

        def big_view(t_in, n):
            return bass.AP(t_in.tensor, 0, [[128, 128], [128 * 128, n], [1, 128]])

        ld0 = self.cnt["ld"]
        sc0 = self.cnt["sc"]
        with nc.Block() as blk:
            @blk.sync
            def _(e):
                self.sem_inc(e.dma_start(TSG[:, :], big_view(self.tl2_in, 48)), "ld", 16)
                self.sem_inc(e.dma_start(TSG2[:, 0:256], big_view(self.tl1_in, 2)), "ld", 16)
                self.sem_inc(e.dma_start(TSG2[:, 256:512], big_view(self.til1_in, 2)), "ld", 16)
                e.wait_ge(self.sem["ld"], self.cnt["ld"])

            @blk.scalar
            def _(e):
                e.wait_ge(self.sem["ld"], ld0 + 48)
                for t in range(48):
                    self.sem_inc(e.activation(self.TB_L2[:, t * 128:(t + 1) * 128],
                                              TSG[:, t * 128:(t + 1) * 128],
                                              AF.Sin, scale=math.pi / P4), "sc")
                for c in range(2):
                    self.sem_inc(e.activation(self.TB_L1[:, c * 128:(c + 1) * 128],
                                              TSG2[:, c * 128:(c + 1) * 128],
                                              AF.Sin, scale=math.pi / P4), "sc")
                for c in range(2):
                    self.sem_inc(e.activation(TMPF[:, c * 128:(c + 1) * 128],
                                              TSG2[:, 256 + c * 128:256 + (c + 1) * 128],
                                              AF.Sin, scale=math.pi / P4), "sc")

            @blk.vector
            def _(e):
                e.wait_ge(self.sem["sc"], sc0 + 52)
                for c in range(2):
                    self.sem_inc(e.tensor_scalar(
                        self.TB_IL1[:, c * 128:(c + 1) * 128],
                        TMPF[:, c * 128:(c + 1) * 128], 2.0 / P4, None, ALU.mult), "ev")

        with nc.Block() as blk:
            @blk.sync
            def _(e):
                self.sem_inc(e.dma_start(TSG[:, :], big_view(self.til2_in, 48)), "ld", 16)
                e.wait_ge(self.sem["ld"], self.cnt["ld"])

            @blk.scalar
            def _(e):
                e.wait_ge(self.sem["ld"], self.cnt["ld"])
                for t in range(48):
                    self.sem_inc(e.activation(self.TB_IL2[:, t * 128:(t + 1) * 128],
                                              TSG[:, t * 128:(t + 1) * 128],
                                              AF.Sin, scale=math.pi / P4), "sc")

        stg_es.close()

    def conv_fwd_half(self, src_buf, src_col, dh, pre_tensor_waits=()):
        nc = self.nc
        P1, P2 = self.P1, self.P2

        def evac_waits(e, base_ev, base_sc, idx):
            if idx % 2 == 0:
                e.wait_ge(self.sem["ev"], base_ev + idx // 2 + 1)
            else:
                e.wait_ge(self.sem["sc"], base_sc + idx // 2 + 1)

        pe0, ev0, sc0 = self.cnt["pe"], self.cnt["ev"], self.cnt["sc"]
        with nc.Block() as blk:
            @blk.tensor
            def _(e):
                for n, v in pre_tensor_waits:
                    e.wait_ge(self.sem[n], v)
                for idx in range(32):
                    c, a = idx // 16, idx % 16
                    if idx >= 4:
                        evac_waits(e, ev0, sc0, idx - 4)
                    mm = e.matmul(self.pmm[idx % 4][:, :],
                                  self.TB_L1[:, c * 128:(c + 1) * 128],
                                  src_buf[:, src_col(a):src_col(a) + 512],
                                  start=True, stop=True)
                    self.sem_inc(mm, "pe")

            def evac_body(e, parity, sem_name):
                for idx in range(parity, 32, 2):
                    c, a = idx // 16, idx % 16
                    e.wait_ge(self.sem["pe"], pe0 + idx + 1)
                    dst = P1[:, c * 8192 + a * 512:c * 8192 + (a + 1) * 512]
                    if parity == 0:
                        self.sem_inc(e.tensor_copy(dst, self.pmm[idx % 4][:, :]), sem_name)
                    else:
                        self.sem_inc(e.activation(dst, self.pmm[idx % 4][:, :], AF.Copy),
                                     sem_name)

            @blk.vector
            def _(e):
                evac_body(e, 0, "ev")

            @blk.scalar
            def _(e):
                evac_body(e, 1, "sc")

            @blk.gpsimd
            def _(e):
                e.wait_ge(self.sem["ev"], ev0 + 16)
                e.wait_ge(self.sem["sc"], sc0 + 16)
                for c in range(2):
                    srcap = bass.AP(P1, c * 8192, [[16384, 128], [512, 16], [1, 512]])
                    dst = bass.AP(self.hs1_t, c * (16 * 128 * DH),
                                  [[512, 128], [65536, 16], [1, 512]])
                    self.sem_inc(e.dma_start(dst, srcap), "gl", 16)
                e.wait_ge(self.sem["gl"], self.cnt["gl"])

            @blk.sync
            def _(e):
                e.wait_ge(self.sem["gl"], self.cnt["gl"])
                for c in range(2):
                    srcap = bass.AP(self.hs1_t, c * (16 * 128 * DH),
                                    [[16 * DH, 128], [1, 16 * DH]])
                    self.sem_inc(e.dma_start(P2[:, c * 8192:(c + 1) * 8192], srcap), "ld", 16)
                e.wait_ge(self.sem["ld"], self.cnt["ld"])

        pe0, ev0, sc0 = self.cnt["pe"], self.cnt["ev"], self.cnt["sc"]
        with nc.Block() as blk:
            @blk.tensor
            def _(e):
                for idx in range(32):
                    cout, g = idx // 16, idx % 16
                    if idx >= 4:
                        evac_waits(e, ev0, sc0, idx - 4)
                    bank = self.pmm[idx % 4][:, :]
                    if cout == 0:
                        e.matmul(bank, self.TB_L2[:, (0 * 16 + g) * 128:(0 * 16 + g + 1) * 128],
                                 P2[:, 0 * 8192 + g * 512:0 * 8192 + (g + 1) * 512],
                                 start=True, stop=False)
                        mm = e.matmul(bank, self.TB_L2[:, (1 * 16 + g) * 128:(1 * 16 + g + 1) * 128],
                                      P2[:, 1 * 8192 + g * 512:1 * 8192 + (g + 1) * 512],
                                      start=False, stop=True)
                    else:
                        e.matmul(bank, self.TB_L2[:, (2 * 16 + g) * 128:(2 * 16 + g + 1) * 128],
                                 P2[:, 0 * 8192 + g * 512:0 * 8192 + (g + 1) * 512],
                                 start=True, stop=False)
                        mm = e.matmul(bank, self.TB_L2[:, (0 * 16 + g) * 128:(0 * 16 + g + 1) * 128],
                                      P2[:, 1 * 8192 + g * 512:1 * 8192 + (g + 1) * 512],
                                      start=False, stop=True)
                    self.sem_inc(mm, "pe")

            def evac_body2(e, parity, sem_name):
                for idx in range(parity, 32, 2):
                    cout, g = idx // 16, idx % 16
                    e.wait_ge(self.sem["pe"], pe0 + idx + 1)
                    dst = P1[:, cout * 8192 + g * 512:cout * 8192 + (g + 1) * 512]
                    if parity == 0:
                        self.sem_inc(e.tensor_copy(dst, self.pmm[idx % 4][:, :]), sem_name)
                    else:
                        self.sem_inc(e.activation(dst, self.pmm[idx % 4][:, :], AF.Copy),
                                     sem_name)

            @blk.vector
            def _(e):
                evac_body2(e, 0, "ev")

            @blk.scalar
            def _(e):
                evac_body2(e, 1, "sc")

    def conv_products(self, st, dh):
        nc = self.nc
        P1, P2 = self.P1, self.P2
        ev0, gp0 = self.cnt["ev"], self.cnt["gp"]
        s0 = {"sA": self.cnt["sA"], "sB": self.cnt["sB"]}
        qslot = lambda q: "sA" if q % 2 == 0 else "sB"
        with nc.Block() as blk:
            @blk.sync
            def _(e):
                for q in range(4):
                    if q >= 2:
                        e.wait_ge(self.sem["ev"], ev0 + 4 * (q - 2) + 4)
                        e.wait_ge(self.sem["gp"], gp0 + 4 * (q - 2) + 4)
                    for c in range(2):
                        self.sem_inc(e.dma_start(
                            GGBQ(self, q, c),
                            self.gts[st, c, dh][:, 4 * q * 512:4 * q * 512 + 2048]),
                            qslot(q), 16)
                e.wait_ge(self.sem["sA"], self.cnt["sA"])
                e.wait_ge(self.sem["sB"], self.cnt["sB"])

            def tiles_of(g):
                q = g // 4
                base = (q % 2) * 4096 + (g % 4) * 512
                xre = P1[:, 0 * 8192 + g * 512:0 * 8192 + (g + 1) * 512]
                xim = P1[:, 1 * 8192 + g * 512:1 * 8192 + (g + 1) * 512]
                gre = self.GGB[:, base:base + 512]
                gim = self.GGB[:, 2048 + base:2048 + base + 512]
                zre = P2[:, 0 * 8192 + g * 512:0 * 8192 + (g + 1) * 512]
                zim = P2[:, 1 * 8192 + g * 512:1 * 8192 + (g + 1) * 512]
                return xre, xim, gre, gim, zre, zim

            @blk.vector
            def _(e):
                for g in range(16):
                    q = g // 4
                    if g % 4 == 0:
                        e.wait_ge(self.sem[qslot(q)], s0[qslot(q)] + 32 * (q // 2 + 1))
                    xre, xim, gre, gim, zre, zim = tiles_of(g)
                    e.tensor_mul(self.PR1[:, :], xre, gre)
                    i1 = e.tensor_mul(self.PR2[:, :], xim, gim)
                    e.wait_ge(self.sem["rwv"], self.sem_inc(i1, "rwv"))
                    self.sem_inc(e.tensor_sub(zre, self.PR1[:, :], self.PR2[:, :]), "ev")
                    e.wait_ge(self.sem["ev"], self.cnt["ev"])

            @blk.gpsimd
            def _(e):
                for g in range(16):
                    q = g // 4
                    if g % 4 == 0:
                        e.wait_ge(self.sem[qslot(q)], s0[qslot(q)] + 32 * (q // 2 + 1))
                    xre, xim, gre, gim, zre, zim = tiles_of(g)
                    e.tensor_mul(self.PR3[:, :], xre, gim)
                    i3 = e.tensor_mul(self.PR4[:, :], xim, gre)
                    e.wait_ge(self.sem["rwg"], self.sem_inc(i3, "rwg"))
                    self.sem_inc(e.tensor_add(zim, self.PR3[:, :], self.PR4[:, :]), "gp")
                    e.wait_ge(self.sem["gp"], self.cnt["gp"])

    def conv_inv_half(self, dh, gate_cb):
        nc = self.nc
        P1, P2 = self.P1, self.P2

        pe0, ev0, sc0 = self.cnt["pe"], self.cnt["ev"], self.cnt["sc"]
        with nc.Block() as blk:
            @blk.tensor
            def _(e):
                for idx in range(32):
                    cout, g = idx // 16, idx % 16
                    if idx >= 4:
                        if (idx - 4) % 2 == 0:
                            e.wait_ge(self.sem["ev"], ev0 + (idx - 4) // 2 + 1)
                        else:
                            e.wait_ge(self.sem["sc"], sc0 + (idx - 4) // 2 + 1)
                    bank = self.pmm[idx % 4][:, :]
                    if cout == 0:
                        e.matmul(bank, self.TB_IL2[:, (0 * 16 + g) * 128:(0 * 16 + g + 1) * 128],
                                 P2[:, g * 512:(g + 1) * 512], start=True, stop=False)
                        mm = e.matmul(bank, self.TB_IL2[:, (2 * 16 + g) * 128:(2 * 16 + g + 1) * 128],
                                      P2[:, 8192 + g * 512:8192 + (g + 1) * 512],
                                      start=False, stop=True)
                    else:
                        e.matmul(bank, self.TB_IL2[:, (1 * 16 + g) * 128:(1 * 16 + g + 1) * 128],
                                 P2[:, g * 512:(g + 1) * 512], start=True, stop=False)
                        mm = e.matmul(bank, self.TB_IL2[:, (0 * 16 + g) * 128:(0 * 16 + g + 1) * 128],
                                      P2[:, 8192 + g * 512:8192 + (g + 1) * 512],
                                      start=False, stop=True)
                    self.sem_inc(mm, "pe")

            def evac_body(e, parity, sem_name):
                for idx in range(parity, 32, 2):
                    cout, g = idx // 16, idx % 16
                    e.wait_ge(self.sem["pe"], pe0 + idx + 1)
                    dst = P1[:, cout * 8192 + g * 512:cout * 8192 + (g + 1) * 512]
                    if parity == 0:
                        self.sem_inc(e.tensor_copy(dst, self.pmm[idx % 4][:, :]), sem_name)
                    else:
                        self.sem_inc(e.activation(dst, self.pmm[idx % 4][:, :], AF.Copy),
                                     sem_name)

            @blk.vector
            def _(e):
                evac_body(e, 0, "ev")

            def leg1(e, c):
                for ap_ in range(16):
                    srcap = bass.AP(P1, (ap_ * 8) * 16384 + c * 8192,
                                    [[16384, 8], [512, 16], [1, 512]])
                    dst = bass.AP(self.hs2_t, c * (128 * 16 * DH) + ap_ * DH,
                                  [[16 * 16 * DH, 8], [16 * DH, 16], [1, DH]])
                    self.sem_inc(e.dma_start(dst, srcap), "gl" if c == 0 else "st", 16)

            @blk.gpsimd
            def _(e):
                e.wait_ge(self.sem["ev"], ev0 + 16)
                e.wait_ge(self.sem["sc"], sc0 + 16)
                leg1(e, 0)
                e.wait_ge(self.sem["gl"], self.cnt["gl"])

            @blk.scalar
            def _(e2):
                evac_body(e2, 1, "sc")
                e2.wait_ge(self.sem["ev"], ev0 + 16)
                leg1(e2, 1)
                e2.wait_ge(self.sem["st"], self.cnt["st"])

            @blk.sync
            def _(e):
                e.wait_ge(self.sem["gl"], self.cnt["gl"])
                e.wait_ge(self.sem["st"], self.cnt["st"])
                for c in range(2):
                    srcap = bass.AP(self.hs2_t, c * (128 * 16 * DH),
                                    [[16 * DH, 128], [1, 16 * DH]])
                    self.sem_inc(e.dma_start(P2[:, c * 8192:(c + 1) * 8192], srcap), "ld", 16)
                e.wait_ge(self.sem["ld"], self.cnt["ld"])

        self.gate_ev_base = self.cnt["ev"]
        pe0 = self.cnt["pe"]
        with nc.Block() as blk:
            @blk.tensor
            def _(e):
                for ap_ in range(16):
                    if ap_ >= 2:
                        e.wait_ge(self.sem["ev"], self.gate_ev_base + (ap_ - 2) + 1)
                    bank = self.pmm[4 + ap_ % 2][:, :]
                    e.matmul(bank, self.TB_IL1[:, 0:128],
                             P2[:, ap_ * 512:(ap_ + 1) * 512], start=True, stop=False)
                    mm = e.matmul(bank, self.TB_IL1[:, 128:256],
                                  P2[:, 8192 + ap_ * 512:8192 + (ap_ + 1) * 512],
                                  start=False, stop=True)
                    self.sem_inc(mm, "pe")

            @blk.sync
            def _(e):
                self.gate_sync(e)

            @blk.vector
            def _(e):
                for ap_ in range(16):
                    e.wait_ge(self.sem["pe"], pe0 + ap_ + 1)
                    gate_cb(e, ap_, self.pmm[4 + ap_ % 2][:, :])

    def phase_filter_prep(self):
        nc = self.nc
        with contextlib.ExitStack() as pes:
            psb = lambda name, shape, dt: pes.enter_context(nc.sbuf_tensor(name, shape, dt))
            FV = psb("FV", [128, 16 * D], BF16)
            self.P1 = psb("P1f", [128, 16384], BF16)
            self.P2 = psb("P2f", [128, 16384], BF16)

            for st in range(2):
                gl0 = self.cnt["gl"]
                with nc.Block() as blk:
                    @blk.gpsimd
                    def _(e):
                        for a in range(NPH):
                            self.sem_inc(e.dma_start(FV[:, a * D:(a + 1) * D],
                                                     self.ft_in[st, a::16, :]), "gl", 16)
                for dh in range(2):
                    self.conv_fwd_half(FV, lambda a, dh=dh: a * D + dh * DH, dh,
                                       pre_tensor_waits=[("gl", gl0 + 16 * NPH)])
                    with nc.Block() as blk:
                        @blk.sync
                        def _(e):
                            for c in range(2):
                                self.sem_inc(e.dma_start(self.gts[st, c, dh],
                                                         self.P1[:, c * 8192:(c + 1) * 8192]),
                                             "st", 16)
                            e.wait_ge(self.sem["st"], self.cnt["st"])
            del self.P1, self.P2

    def ln_transpose(self, tile_specs, xf_dst):
        nc = self.nc
        n = len(tile_specs)
        pe0, ev0, sc0, te0 = (self.cnt[k] for k in ("pe", "ev", "sc", "te"))
        s0 = {"sA": self.cnt["sA"], "sB": self.cnt["sB"]}
        slot_of = lambda i: "sA" if i % 2 == 0 else "sB"
        slot_w = lambda i: s0[slot_of(i)] + 16 * (i // 2 + 1)
        with nc.Block() as blk:
            @blk.sync
            def _(e):
                for i in range(n):
                    if i >= 2:
                        e.wait_ge(self.sem["ev"], ev0 + 3 * (i - 2) + 3)
                        e.wait_ge(self.sem["sc"], sc0 + 5 * (i - 2) + 1)
                    self.sem_inc(e.dma_start(self.XT[:, (i % 2) * D:(i % 2 + 1) * D],
                                             tile_specs[i][0]), slot_of(i), 16)
                e.wait_ge(self.sem["sA"], self.cnt["sA"])
                e.wait_ge(self.sem["sB"], self.cnt["sB"])

            @blk.tensor
            def _(e):
                e.wait_ge(self.sem["gl"], self.gl_ident)
                for i in range(n):
                    bix = i % 2
                    e.wait_ge(self.sem["ev"], ev0 + 3 * i + 3)
                    if i >= 1:
                        e.wait_ge(self.sem["te"], te0 + i)
                    for k in range(8):
                        mm = e.transpose(self.ptr[:, k * 128:(k + 1) * 128],
                                         self.LNT[:, bix * D + k * 128:bix * D + (k + 1) * 128],
                                         self.IDENT[:, :])
                        if k == 7:
                            self.sem_inc(mm, "pe")

            @blk.vector
            def _(e):
                for i in range(n):
                    bix = i % 2
                    xt = self.XT[:, bix * D:(bix + 1) * D]
                    c = lambda role: self.STT[:, bix * 8 + role:bix * 8 + role + 1]
                    e.wait_ge(self.sem[slot_of(i)], slot_w(i))
                    if i >= 2:
                        e.wait_ge(self.sem["sc"], sc0 + 5 * (i - 2) + 3)
                    self.sem_inc(e.tensor_reduce(c(0), xt, AX.X, ALU.add), "ev")
                    e.wait_ge(self.sem["sc"], sc0 + 5 * i + 3)
                    self.sem_inc(e.scalar_tensor_tensor(c(5), c(1), 1.0 / D, c(3),
                                                        ALU.mult, ALU.subtract), "ev")
                    e.wait_ge(self.sem["sc"], sc0 + 5 * i + 5)
                    if i >= 2:
                        e.wait_ge(self.sem["pe"], pe0 + (i - 2) + 1)
                    self.sem_inc(e.tensor_scalar(self.LNT[:, bix * D:(bix + 1) * D], xt,
                                                 c(2), c(7), ALU.subtract, ALU.mult), "ev")
                    if i >= 1:
                        e.wait_ge(self.sem["pe"], pe0 + i)
                        self.sem_inc(e.tensor_copy(
                            xf_dst[:, tile_specs[i - 1][1]:tile_specs[i - 1][1] + 1024],
                            self.ptr[:, :]), "te")
                e.wait_ge(self.sem["pe"], pe0 + n)
                self.sem_inc(e.tensor_copy(
                    xf_dst[:, tile_specs[n - 1][1]:tile_specs[n - 1][1] + 1024],
                    self.ptr[:, :]), "te")

            @blk.scalar
            def _(e):
                for i in range(n):
                    bix = i % 2
                    xt = self.XT[:, bix * D:(bix + 1) * D]
                    c = lambda role: self.STT[:, bix * 8 + role:bix * 8 + role + 1]
                    e.wait_ge(self.sem[slot_of(i)], slot_w(i))
                    self.sem_inc(e.activation(self.SQ[:, :], xt, AF.Square,
                                              accum_out=c(1)), "sc")
                    e.wait_ge(self.sem["ev"], ev0 + 3 * i + 1)
                    self.sem_inc(e.activation(c(2), c(0), AF.Copy, scale=1.0 / D), "sc")
                    self.sem_inc(e.activation(c(3), c(0), AF.Square, scale=1.0 / D), "sc")
                    e.wait_ge(self.sem["ev"], ev0 + 3 * i + 2)
                    self.sem_inc(e.activation(c(6), c(5), AF.Ln, bias=self.EPS[:, :]), "sc")
                    e.wait_ge(self.sem["sc"], self.cnt["sc"])
                    self.sem_inc(e.activation(c(7), c(6), AF.Exp, scale=-0.5), "sc")

    def phase_batch(self, b, es):
        nc = self.nc
        with contextlib.ExitStack() as bes:
            bsb = lambda name, shape, dt: bes.enter_context(nc.sbuf_tensor(name, shape, dt))
            VA = bsb(f"VA{b}", [128, 16 * D], BF16)

            with contextlib.ExitStack() as ses:
                ssb = lambda name, shape, dt: ses.enter_context(nc.sbuf_tensor(name, shape, dt))
                XF = ssb(f"XF{b}", [128, 16 * D], BF16)
                WW = ssb(f"WW{b}", [128, 8 * 3 * D], BF16)
                PBREP = ssb(f"PBREP{b}", [128, 3 * D], F32)
                GS = ssb(f"GS{b}", [128, 4 * D], BF16)

                gl0 = self.cnt["gl"]
                with nc.Block() as blk:
                    @blk.gpsimd
                    def _(e):
                        for ec in range(8):
                            self.sem_inc(e.dma_start(WW[:, ec * 3072:(ec + 1) * 3072],
                                                     self.pw_in[ec * 128:(ec + 1) * 128, :]),
                                         "gl", 16)

                    @blk.sync
                    def _(e):
                        self.sem_inc(e.dma_start(
                            PBREP[:, :], self.pb_in[None, :].broadcast_to([128, 3 * D])),
                            "ld", 16)

                self.ln_transpose(
                    [(self.x_in[b, a::16, :], a * D) for a in range(NPH)], XF)
                self.maybe_dump(f"xf{b}", XF[:, :], [128, 16 * D], BF16)

                pe0, ev0 = self.cnt["pe"], self.cnt["ev"]
                t0 = {"tA": self.cnt["tA"], "tB": self.cnt["tB"]}
                tslot = lambda a: "tA" if a % 2 == 0 else "tB"
                ld_pb = self.cnt["ld"]
                gl_ww = gl0 + 16 * 8
                with nc.Block() as blk:
                    @blk.tensor
                    def _(e):
                        e.wait_ge(self.sem["gl"], gl_ww)
                        for a in range(NPH):
                            for h in range(2):
                                gidx = a * 2 + h
                                if gidx >= 2:
                                    e.wait_ge(self.sem["ev"], ev0 + 3 * (gidx - 1))
                                for ec in range(8):
                                    for j in range(3):
                                        jj = h * 3 + j
                                        mm = e.matmul(
                                            self.pmm[jj][:, :],
                                            XF[:, a * D + ec * 128:a * D + (ec + 1) * 128],
                                            WW[:, ec * 3072 + jj * 512:ec * 3072 + (jj + 1) * 512],
                                            start=(ec == 0), stop=(ec == 7))
                                self.sem_inc(mm, "pe")

                    @blk.vector
                    def _(e):
                        e.wait_ge(self.sem["ld"], ld_pb)
                        for a in range(NPH):
                            for h in range(2):
                                gidx = a * 2 + h
                                e.wait_ge(self.sem["pe"], pe0 + gidx + 1)
                                if gidx >= 4:
                                    e.wait_ge(self.sem[tslot(a)],
                                              t0[tslot(a)] + 32 * ((a - 2) // 2 + 1))
                                for j in range(3):
                                    jj = h * 3 + j
                                    src = self.pmm[jj][:, :]
                                    brp = PBREP[:, jj * 512:(jj + 1) * 512]
                                    if jj < 2:
                                        dst = VA[:, a * D + jj * 512:a * D + (jj + 1) * 512]
                                    elif jj < 4:
                                        dst = GS[:, (a % 2) * D + (jj - 2) * 512:
                                                 (a % 2) * D + (jj - 1) * 512]
                                    else:
                                        dst = GS[:, 2048 + (a % 2) * D + (jj - 4) * 512:
                                                 2048 + (a % 2) * D + (jj - 3) * 512]
                                    self.sem_inc(e.tensor_tensor(dst, src, brp, ALU.add), "ev")

                    @blk.sync
                    def _(e):
                        for a in range(NPH):
                            e.wait_ge(self.sem["ev"], ev0 + 6 * (a + 1))
                            self.sem_inc(e.dma_start(self.g1sp[b, a],
                                                     GS[:, (a % 2) * D:(a % 2 + 1) * D]),
                                         tslot(a), 16)
                            self.sem_inc(e.dma_start(self.g2sp[b, a],
                                                     GS[:, 2048 + (a % 2) * D:2048 + (a % 2 + 1) * D]),
                                         tslot(a), 16)
                        e.wait_ge(self.sem["tA"], self.cnt["tA"])
                        e.wait_ge(self.sem["tB"], self.cnt["tB"])
                self.maybe_dump(f"va{b}", VA[:, :], [128, 16 * D], BF16)

            if self.batch_stop == "proj":
                return
            with contextlib.ExitStack() as ces:
                csb = lambda name, shape, dt: ces.enter_context(nc.sbuf_tensor(name, shape, dt))
                VB = csb(f"VB{b}", [128, 16 * D], BF16)
                self.P1 = csb(f"P1b{b}", [128, 16384], BF16)
                self.P2 = csb(f"P2b{b}", [128, 16384], BF16)
                self.GGB = csb(f"GGB{b}", [128, 8192], BF16)
                self.PR1 = csb(f"PR1b{b}", [128, 512], BF16)
                self.PR2 = csb(f"PR2b{b}", [128, 512], BF16)
                self.PR3 = csb(f"PR3b{b}", [128, 512], BF16)
                self.PR4 = csb(f"PR4b{b}", [128, 512], BF16)
                GT = csb(f"GT{b}", [128, 2048], BF16)
                TMPG = csb(f"TMPG{b}", [128, 512], F32)
                X1H = csb(f"X1H{b}", [128, 2048], F32)

                nstage = 1 if self.batch_stop == "s1" else 2
                for st in range(nstage):
                    src = VA if st == 0 else VB
                    gsp = self.g1sp if st == 0 else self.g2sp
                    for dh in range(2):
                        self.conv_fwd_half(src, lambda a, dh=dh: a * D + dh * DH, dh)
                        self.conv_products(st, dh)

                        gate_state = {}

                        gslot = lambda grp: "sA" if grp % 2 == 0 else "sB"
                        gtslot = lambda grp: "tA" if grp % 2 == 0 else "tB"
                        per = 16 if st == 0 else 32
                        g1t = self.g1sp.tensor if hasattr(self.g1sp, "tensor") else None

                        def gate_sync(e, b=b, st=st, dh=dh, gsp=gsp):
                            gate_state["s0"] = {"sA": self.cnt["sA"], "sB": self.cnt["sB"]}
                            gate_state["t0"] = {"tA": self.cnt["tA"], "tB": self.cnt["tB"]}
                            evb = self.gate_ev_base
                            gspt = gsp.tensor
                            xt_t = self.x_in.tensor

                            def store_pair(k):
                                dst = bass.AP(self.x1sp.tensor,
                                              (b * 16 + 2 * k) * 131072 + dh * DH,
                                              [[1024, 128], [131072, 2], [1, 512]])
                                srcap = bass.AP(X1H, (k % 2) * 1024,
                                                [[2048, 128], [512, 2], [1, 512]])
                                self.sem_inc(e.dma_start(dst, srcap), gtslot(k), 16)

                            for grp in range(8):
                                if grp >= 2:
                                    e.wait_ge(self.sem["ev"], evb + 2 * (grp - 2) + 2)
                                    if st == 1:
                                        store_pair(grp - 2)
                                gt_src = bass.AP(gspt,
                                                 (b * 16 + 2 * grp) * 131072 + dh * DH,
                                                 [[1024, 128], [131072, 2], [1, 512]])
                                self.sem_inc(e.dma_start(
                                    GT[:, (grp % 2) * 1024:(grp % 2 + 1) * 1024]
                                    .rearrange("p (a d) -> p a d", a=2), gt_src),
                                    gslot(grp), 16)
                                if st == 1:
                                    x_src = bass.AP(xt_t,
                                                    (b * 2048 + 2 * grp) * 1024 + dh * DH,
                                                    [[16 * 1024, 128], [1024, 2], [1, 512]])
                                    self.sem_inc(e.dma_start(
                                        self.XT[:, (grp % 2) * 1024:(grp % 2 + 1) * 1024]
                                        .rearrange("p (a d) -> p a d", a=2), x_src),
                                        gslot(grp), 16)
                            if st == 1:
                                for k in (6, 7):
                                    e.wait_ge(self.sem["ev"], evb + 2 * k + 2)
                                    store_pair(k)
                                e.wait_ge(self.sem["tA"], self.cnt["tA"])
                                e.wait_ge(self.sem["tB"], self.cnt["tB"])
                            e.wait_ge(self.sem["sA"], self.cnt["sA"])
                            e.wait_ge(self.sem["sB"], self.cnt["sB"])

                        def gt_slice(ap_):
                            grp = ap_ // 2
                            base = (grp % 2) * 1024 + (ap_ % 2) * 512
                            return GT[:, base:base + 512]

                        def xt_slice(ap_):
                            grp = ap_ // 2
                            base = (grp % 2) * 1024 + (ap_ % 2) * 512
                            return self.XT[:, base:base + 512]

                        def x1h_slice(ap_):
                            grp = ap_ // 2
                            base = (grp % 2) * 1024 + (ap_ % 2) * 512
                            return X1H[:, base:base + 512]

                        if st == 0:
                            def gate_cb(e, ap_, psum, dh=dh):
                                grp = ap_ // 2
                                sl = gslot(grp)
                                e.wait_ge(self.sem[sl],
                                          gate_state["s0"][sl] + per * (grp // 2 + 1))
                                self.sem_inc(e.tensor_tensor(
                                    VB[:, ap_ * D + dh * DH:ap_ * D + (dh + 1) * DH],
                                    psum, gt_slice(ap_), ALU.mult), "ev")
                        else:
                            def gate_cb(e, ap_, psum, dh=dh):
                                grp = ap_ // 2
                                sl = gslot(grp)
                                e.wait_ge(self.sem[sl],
                                          gate_state["s0"][sl] + per * (grp // 2 + 1))
                                if grp >= 2:
                                    tl = gtslot(grp)
                                    e.wait_ge(self.sem[tl],
                                              gate_state["t0"][tl] + 16 * ((grp - 2) // 2 + 1))
                                i1 = e.tensor_tensor(TMPG[:, :], psum, gt_slice(ap_),
                                                     ALU.mult)
                                e.wait_ge(self.sem["rwv"], self.sem_inc(i1, "rwv"))
                                self.sem_inc(e.tensor_tensor(
                                    x1h_slice(ap_), TMPG[:, :], xt_slice(ap_),
                                    ALU.add), "ev")
                                e.wait_ge(self.sem["ev"], self.cnt["ev"])

                        self.gate_sync = gate_sync
                        self.conv_inv_half(dh, gate_cb)

                self.maybe_dump(f"vb{b}", VB[:, :], [128, 16 * D], BF16)
                del self.P1, self.P2

            cnt = dict(self.cnt)
            with nc.Block() as blk:
                @blk.sync
                def _(e):
                    for n in ("ld", "st", "sA", "sB", "tA", "tB"):
                        if cnt[n]:
                            e.wait_ge(self.sem[n], cnt[n])

                @blk.gpsimd
                def _(e):
                    e.wait_ge(self.sem["gl"], cnt["gl"])

    def phase_ffn(self, es):
        nc = self.nc
        with contextlib.ExitStack() as fes:
            fsb = lambda name, shape, dt: fes.enter_context(nc.sbuf_tensor(name, shape, dt))
            W1S = fsb("W1S", [128, 8 * 4096], BF16)
            W2S = fsb("W2S", [128, 32 * D], BF16)
            HT = fsb("HT", [128, 32 * 512], BF16)
            LN2FM = fsb("LN2FM", [128, 4 * D], BF16)
            B1C = fsb("B1C", [128, 32], F32)
            B2R = fsb("B2R", [128, D], F32)
            X1T = fsb("X1T", [128, 2 * D], F32)
            OTS = fsb("OTS", [128, 2048], F32)

            gl0 = self.cnt["gl"]
            with nc.Block() as blk:
                @blk.gpsimd
                def _(e):
                    for ec in range(8):
                        self.sem_inc(e.dma_start(W1S[:, ec * 4096:(ec + 1) * 4096],
                                                 self.w1_in[ec * 128:(ec + 1) * 128, :]),
                                     "gl", 16)
                    for hc in range(32):
                        self.sem_inc(e.dma_start(W2S[:, hc * D:(hc + 1) * D],
                                                 self.w2_in[hc * 128:(hc + 1) * 128, :]),
                                     "gl", 16)

                @blk.sync
                def _(e):
                    self.sem_inc(e.dma_start(
                        B1C[:, :], self.b1_in.rearrange("(c p) -> p c", p=128),
                        allow_slow_non_contiguous=True), "ld", 16)
                    self.sem_inc(e.dma_start(
                        B2R[:, :], self.b2_in[None, :].broadcast_to([128, D])), "ld", 16)
                    e.wait_ge(self.sem["ld"], self.cnt["ld"])
            gl_w = self.cnt["gl"]

            if self.ffn_stop == "load":
                return
            for b in range(NB):
                for ch in range(4):
                    tiles = [4 * ch + i for i in range(4)]
                    self.ln_transpose(
                        [(self.x1sp[b, t], i * D) for i, t in enumerate(tiles)], LN2FM)
                    if self.ffn_stop == "ln":
                        continue

                    pe0, sc0 = self.cnt["pe"], self.cnt["sc"]
                    with nc.Block() as blk:
                        @blk.tensor
                        def _(e):
                            e.wait_ge(self.sem["gl"], gl_w)
                            for hc in range(32):
                                if hc >= 2:
                                    e.wait_ge(self.sem["sc"], sc0 + hc - 1)
                                for ec in range(8):
                                    rhs = bass.AP(LN2FM, ec * 128,
                                                  [[4 * D, 128], [D, 4], [1, 128]])
                                    mm = e.matmul(self.pmm[hc % 2][:, :],
                                                  W1S[:, ec * 4096 + hc * 128:
                                                      ec * 4096 + (hc + 1) * 128],
                                                  rhs, start=(ec == 0), stop=(ec == 7))
                                self.sem_inc(mm, "pe")

                        @blk.scalar
                        def _(e):
                            for hc in range(32):
                                e.wait_ge(self.sem["pe"], pe0 + hc + 1)
                                self.sem_inc(e.activation(
                                    HT[:, hc * 512:(hc + 1) * 512],
                                    self.pmm[hc % 2][:, :], AF.Gelu,
                                    bias=B1C[:, hc:hc + 1]), "sc")

                    if self.ffn_stop == "ffn1":
                        continue
                    if self.ffn_stop == "ffn2first" and (b, ch) != (0, 0):
                        continue
                    pe0, ev0 = (self.cnt[k] for k in ("pe", "ev"))
                    s0f = {"sA": self.cnt["sA"], "sB": self.cnt["sB"]}
                    t0f = {"tA": self.cnt["tA"], "tB": self.cnt["tB"]}
                    fslot = lambda i: "sA" if i % 2 == 0 else "sB"
                    ftslot = lambda gi: "tA" if gi % 2 == 0 else "tB"
                    with nc.Block() as blk:
                        @blk.sync
                        def _(e):
                            for i in (0, 1):
                                self.sem_inc(e.dma_start(
                                    self.X1T_dst(X1T, i), self.x1sp[b, tiles[i]]),
                                    fslot(i), 16)
                            for gi in range(8):
                                i, j = gi // 2, gi % 2
                                e.wait_ge(self.sem["ev"], ev0 + gi + 1)
                                slot = (gi % 2) * 1024 + 512
                                self.sem_inc(e.dma_start(
                                    self.out[b, tiles[i]::16, j * 512:(j + 1) * 512],
                                    OTS[:, slot:slot + 512]), ftslot(gi), 16)
                                if gi in (1, 3):
                                    ii = 2 + (gi - 1) // 2
                                    self.sem_inc(e.dma_start(
                                        self.X1T_dst(X1T, ii), self.x1sp[b, tiles[ii]]),
                                        fslot(ii), 16)
                            e.wait_ge(self.sem["tA"], self.cnt["tA"])
                            e.wait_ge(self.sem["tB"], self.cnt["tB"])
                            e.wait_ge(self.sem["sA"], self.cnt["sA"])
                            e.wait_ge(self.sem["sB"], self.cnt["sB"])

                        @blk.tensor
                        def _(e):
                            for i in range(4):
                                if i >= 2:
                                    e.wait_ge(self.sem["ev"], ev0 + 2 * (i - 2) + 2)
                                b0 = self.pmm[2 + 2 * (i % 2)][:, :]
                                b1 = self.pmm[3 + 2 * (i % 2)][:, :]
                                for hc in range(32):
                                    lhsT = HT[:, hc * 512 + i * 128:hc * 512 + (i + 1) * 128]
                                    e.matmul(b0, lhsT, W2S[:, hc * D:hc * D + 512],
                                             start=(hc == 0), stop=(hc == 31))
                                    mm = e.matmul(b1, lhsT, W2S[:, hc * D + 512:(hc + 1) * D],
                                                  start=(hc == 0), stop=(hc == 31))
                                self.sem_inc(mm, "pe")

                        @blk.vector
                        def _(e):
                            for gi in range(8):
                                i, j = gi // 2, gi % 2
                                e.wait_ge(self.sem["pe"], pe0 + i + 1)
                                e.wait_ge(self.sem[fslot(i)], s0f[fslot(i)] + 16 * (i // 2 + 1))
                                if gi >= 2:
                                    tl = ftslot(gi)
                                    e.wait_ge(self.sem[tl], t0f[tl] + 16 * ((gi - 2) // 2 + 1))
                                slot = (gi % 2) * 1024
                                i1 = e.tensor_tensor(OTS[:, slot:slot + 512],
                                                     self.pmm[2 + 2 * (i % 2) + j][:, :],
                                                     X1T[:, (i % 2) * D + j * 512:
                                                         (i % 2) * D + (j + 1) * 512],
                                                     ALU.add)
                                e.wait_ge(self.sem["rwv"], self.sem_inc(i1, "rwv"))
                                self.sem_inc(e.tensor_tensor(
                                    OTS[:, slot + 512:slot + 1024],
                                    OTS[:, slot:slot + 512],
                                    B2R[:, j * 512:(j + 1) * 512], ALU.add), "ev")

    def X1T_dst(self, X1T, i):
        return X1T[:, (i % 2) * D:(i % 2 + 1) * D]


_CACHE = {}


def _build():
    if "nc" not in _CACHE:
        b = Builder()
        _CACHE["nc"] = b.build()
    return _CACHE["nc"]


def kernel(L0_attn_x, L0_attn_proj_w, L0_attn_proj_b, L0_attn_filter_time,
           L0_ffn_w1, L0_ffn_b1, L0_ffn_w2, L0_ffn_b2):
    nc = _build()
    t_l1, t_l2, t_il2, t_il1 = host_tables()
    ident = np.eye(128, dtype=np.float32)
    x = np.ascontiguousarray(np.asarray(L0_attn_x, dtype=np.float32))
    ft = np.ascontiguousarray(np.asarray(L0_attn_filter_time, dtype=np.float32)
                              .reshape(2, L, D))
    shared = {
        "pw": np.ascontiguousarray(L0_attn_proj_w, dtype=np.float32),
        "pb": np.ascontiguousarray(L0_attn_proj_b, dtype=np.float32),
        "ft": ft,
        "w1": np.ascontiguousarray(L0_ffn_w1, dtype=np.float32),
        "b1": np.ascontiguousarray(L0_ffn_b1, dtype=np.float32),
        "w2": np.ascontiguousarray(L0_ffn_w2, dtype=np.float32),
        "b2": np.ascontiguousarray(L0_ffn_b2, dtype=np.float32),
        "tl1": t_l1, "tl2": t_l2, "til2": t_il2, "til1": t_il1,
        "idm": ident,
    }
    in_maps = []
    for c in range(NCORES):
        m = dict(shared)
        m["x"] = np.ascontiguousarray(x[c * NB:(c + 1) * NB])
        in_maps.append(m)
    res = run_bass_kernel_spmd(nc, in_maps, core_ids=list(range(NCORES)))
    return np.concatenate([r["out"] for r in res.results], axis=0)


# version 06
# speedup vs baseline: 1.0608x; 1.0608x over previous
import contextlib
import math

import numpy as np

import concourse.bass as bass
import concourse.mybir as mybir
from concourse.bass_utils import run_bass_kernel_spmd

F32 = mybir.dt.float32
BF16 = mybir.dt.bfloat16
AF = mybir.ActivationFunctionType
ALU = mybir.AluOpType
AX = mybir.AxisListType

P4 = 4096
NPH = 16
L = 2048
D = 1024
DH = 512
NB = 2
NCORES = 8


def _red(m):
    r = np.mod(m, 2 * P4)
    r = np.where(r > P4, r - 2 * P4, r)
    return r.astype(np.float32)


def GGBQ(self, q, c):
    base = (q % 2) * 4096 + c * 2048
    return self.GGB[:, base:base + 2048]


def host_tables():
    COS = P4 // 2
    r = np.arange(128)
    kap_of_r = 8 * (r % 16) + r // 16
    b = np.arange(128)
    base1 = 16 * np.outer(b, 2 * kap_of_r + 1)
    t_l1 = np.stack([_red(base1 + COS), _red(-base1)])
    m = np.arange(128)
    basei = 16 * np.outer(2 * kap_of_r + 1, m)
    t_il1 = np.stack([_red(basei + COS), _red(-basei)])
    a = np.arange(NPH)
    t_l2 = np.zeros((3, 16, 128, 128), np.float32)
    t_il2 = np.zeros((3, 16, 128, 128), np.float32)
    for g in range(16):
        for gam in range(8):
            kpp = 8 * g + gam
            rows = a * 8 + gam
            th = a[:, None] * (2 * kpp + 1) + 512 * np.outer(a, a)
            t_l2[0, g, rows[:, None], rows[None, :]] = _red(th + COS)
            t_l2[1, g, rows[:, None], rows[None, :]] = _red(th)
            t_l2[2, g, rows[:, None], rows[None, :]] = _red(-th)
            thi = a[None, :] * (2 * kpp + 1) + 512 * np.outer(a, a)
            t_il2[0, g, rows[:, None], rows[None, :]] = _red(thi + COS)
            t_il2[1, g, rows[:, None], rows[None, :]] = _red(thi)
            t_il2[2, g, rows[:, None], rows[None, :]] = _red(-thi)
    return t_l1, t_l2, t_il2, t_il1


class Builder:
    def __init__(self, debug_dumps=(), stop_after="full", batch_stop=None, ffn_stop=None):
        self.nc = bass.Bass("TRN2", target_bir_lowering=False)
        self.debug_dumps = set(debug_dumps)
        self.stop_after = stop_after
        self.batch_stop = batch_stop
        self.ffn_stop = ffn_stop
        self.prodev = {}
        self.prodgp = {}
        self.dbg = {}

    def sem_inc(self, inst, name, k=1):
        inst.then_inc(self.sem[name], k)
        self.cnt[name] += k
        return self.cnt[name]

    def fence(self, e, inst, name):
        self.sem_inc(inst, name)
        e.wait_ge(self.sem[name], self.cnt[name])

    def maybe_dump(self, tag, src_ap, shape, dtype):
        if tag not in self.debug_dumps:
            return
        t = self.nc.dram_tensor("dbg_" + tag, shape, dtype, kind="ExternalOutput").ap()
        self.dbg[tag] = t
        with self.nc.Block() as blk:
            @blk.sync
            def _(e):
                self.sem_inc(e.dma_start(t, src_ap), "st", 16)
                e.wait_ge(self.sem["st"], self.cnt["st"])

    def build(self):
        nc = self.nc
        self.x_in = nc.dram_tensor("x", [NB, L, D], F32, kind="ExternalInput").ap()
        self.pw_in = nc.dram_tensor("pw", [D, 3 * D], F32, kind="ExternalInput").ap()
        self.pb_in = nc.dram_tensor("pb", [3 * D], F32, kind="ExternalInput").ap()
        self.ft_in = nc.dram_tensor("ft", [2, L, D], F32, kind="ExternalInput").ap()
        self.w1_in = nc.dram_tensor("w1", [D, 4 * D], F32, kind="ExternalInput").ap()
        self.b1_in = nc.dram_tensor("b1", [4 * D], F32, kind="ExternalInput").ap()
        self.w2_in = nc.dram_tensor("w2", [4 * D, D], F32, kind="ExternalInput").ap()
        self.b2_in = nc.dram_tensor("b2", [D], F32, kind="ExternalInput").ap()
        self.tl1_in = nc.dram_tensor("tl1", [2, 128, 128], F32, kind="ExternalInput").ap()
        self.tl2_in = nc.dram_tensor("tl2", [3, 16, 128, 128], F32, kind="ExternalInput").ap()
        self.til2_in = nc.dram_tensor("til2", [3, 16, 128, 128], F32, kind="ExternalInput").ap()
        self.til1_in = nc.dram_tensor("til1", [2, 128, 128], F32, kind="ExternalInput").ap()
        self.id_in = nc.dram_tensor("idm", [128, 128], F32, kind="ExternalInput").ap()
        self.out = nc.dram_tensor("out", [NB, L, D], F32, kind="ExternalOutput").ap()

        self.g1sp = nc.dram_tensor("g1sp", [NB, 16, 128, D], BF16).ap()
        self.g2sp = nc.dram_tensor("g2sp", [NB, 16, 128, D], BF16).ap()
        self.x1sp = nc.dram_tensor("x1sp", [NB, 16, 128, D], F32).ap()
        self.gts = nc.dram_tensor("gts", [2, 2, 2, 128, 16 * DH], BF16).ap()
        self.hs1_t = nc.dram_tensor("hs1", [2, 16, 128, DH], BF16)
        self.hs2_t = nc.dram_tensor("hs2", [2, 128, 16, DH], BF16)
        self.hs1 = self.hs1_t.ap()

        es = contextlib.ExitStack()
        with es:
            sem_names = ["ld", "gl", "st", "pe", "ev", "sc", "te", "sA", "sB", "tA", "tB", "rwv", "rws", "gp", "rwg"]
            self.sem = {n: es.enter_context(nc.semaphore(n)) for n in sem_names}
            self.cnt = {n: 0 for n in sem_names}
            sb = lambda name, shape, dt: es.enter_context(nc.sbuf_tensor(name, shape, dt))

            self.IDENT = sb("IDENT", [128, 128], BF16)
            self.EPS = sb("EPS", [128, 1], F32)
            self.STT = sb("STT", [128, 16], F32)
            self.XT = sb("XT", [128, 2 * D], F32)
            self.SQ = sb("SQ", [128, D], F32)
            self.LNT = sb("LNT", [128, 2 * D], BF16)
            self.pmm = [es.enter_context(nc.psum_tensor(f"pm{i}", [128, 512], F32))
                        for i in range(6)]
            self.ptr = es.enter_context(nc.psum_tensor("ptr", [128, 1024], BF16))

            with nc.Block() as blk:
                @blk.vector
                def _(e):
                    e.memset(self.EPS[:, :], 1e-5)

                @blk.gpsimd
                def _(e):
                    self.sem_inc(e.dma_start(self.IDENT[:, :], self.id_in[:, :]), "gl", 16)
            self.gl_ident = self.cnt["gl"]

            with contextlib.ExitStack() as tes:
                tsb = lambda name, shape, dt: tes.enter_context(nc.sbuf_tensor(name, shape, dt))
                self.TB_L1 = tsb("TB_L1", [128, 256], BF16)
                self.TB_IL1 = tsb("TB_IL1", [128, 256], BF16)
                self.TB_L2 = tsb("TB_L2", [128, 48 * 128], BF16)
                self.TB_IL2 = tsb("TB_IL2", [128, 48 * 128], BF16)
                self.phase_tables(tsb)
                if self.stop_after != "tables":
                    if self.stop_after != "notables_filter":
                        self.phase_filter_prep()
                    if self.stop_after not in ("filter",):
                        for b in range(NB):
                            self.phase_batch(b, es)
                            if self.stop_after == "b0":
                                break
            if self.stop_after == "full":
                self.phase_ffn(es)

            cnt = dict(self.cnt)
            with nc.Block() as blk:
                @blk.sync
                def _(e):
                    for n in ("ld", "st", "sA", "sB", "tA", "tB"):
                        if cnt[n]:
                            e.wait_ge(self.sem[n], cnt[n])

                @blk.gpsimd
                def _(e):
                    e.wait_ge(self.sem["gl"], cnt["gl"])
        return nc

    def phase_tables(self, tsb):
        nc = self.nc
        stg_es = contextlib.ExitStack()
        ssb = lambda name, shape, dt: stg_es.enter_context(nc.sbuf_tensor(name, shape, dt))
        TSG = ssb("TSG", [128, 48 * 128], F32)
        TSG2 = ssb("TSG2", [128, 512], F32)
        TMPF = ssb("TMPF", [128, 256], F32)

        def big_view(t_in, n):
            return bass.AP(t_in.tensor, 0, [[128, 128], [128 * 128, n], [1, 128]])

        ld0 = self.cnt["ld"]
        sc0 = self.cnt["sc"]
        with nc.Block() as blk:
            @blk.sync
            def _(e):
                self.sem_inc(e.dma_start(TSG[:, :], big_view(self.tl2_in, 48)), "ld", 16)
                self.sem_inc(e.dma_start(TSG2[:, 0:256], big_view(self.tl1_in, 2)), "ld", 16)
                self.sem_inc(e.dma_start(TSG2[:, 256:512], big_view(self.til1_in, 2)), "ld", 16)
                e.wait_ge(self.sem["ld"], self.cnt["ld"])

            @blk.scalar
            def _(e):
                e.wait_ge(self.sem["ld"], ld0 + 48)
                for t in range(48):
                    self.sem_inc(e.activation(self.TB_L2[:, t * 128:(t + 1) * 128],
                                              TSG[:, t * 128:(t + 1) * 128],
                                              AF.Sin, scale=math.pi / P4), "sc")
                for c in range(2):
                    self.sem_inc(e.activation(self.TB_L1[:, c * 128:(c + 1) * 128],
                                              TSG2[:, c * 128:(c + 1) * 128],
                                              AF.Sin, scale=math.pi / P4), "sc")
                for c in range(2):
                    self.sem_inc(e.activation(TMPF[:, c * 128:(c + 1) * 128],
                                              TSG2[:, 256 + c * 128:256 + (c + 1) * 128],
                                              AF.Sin, scale=math.pi / P4), "sc")

            @blk.vector
            def _(e):
                e.wait_ge(self.sem["sc"], sc0 + 52)
                for c in range(2):
                    self.sem_inc(e.tensor_scalar(
                        self.TB_IL1[:, c * 128:(c + 1) * 128],
                        TMPF[:, c * 128:(c + 1) * 128], 2.0 / P4, None, ALU.mult), "ev")

        with nc.Block() as blk:
            @blk.sync
            def _(e):
                self.sem_inc(e.dma_start(TSG[:, :], big_view(self.til2_in, 48)), "ld", 16)
                e.wait_ge(self.sem["ld"], self.cnt["ld"])

            @blk.scalar
            def _(e):
                e.wait_ge(self.sem["ld"], self.cnt["ld"])
                for t in range(48):
                    self.sem_inc(e.activation(self.TB_IL2[:, t * 128:(t + 1) * 128],
                                              TSG[:, t * 128:(t + 1) * 128],
                                              AF.Sin, scale=math.pi / P4), "sc")

        stg_es.close()

    def conv_fwd_half(self, src_buf, src_col, dh, pre_tensor_waits=()):
        nc = self.nc
        P1, P2 = self.P1, self.P2

        def evac_waits(e, base_ev, base_sc, idx):
            if idx % 2 == 0:
                e.wait_ge(self.sem["ev"], base_ev + idx // 2 + 1)
            else:
                e.wait_ge(self.sem["sc"], base_sc + idx // 2 + 1)

        pe0, ev0, sc0 = self.cnt["pe"], self.cnt["ev"], self.cnt["sc"]
        gl_leg1_base = self.cnt["gl"]
        with nc.Block() as blk:
            @blk.tensor
            def _(e):
                for n, v in pre_tensor_waits:
                    e.wait_ge(self.sem[n], v)
                for idx in range(32):
                    c, a = idx // 16, idx % 16
                    if idx >= 4:
                        evac_waits(e, ev0, sc0, idx - 4)
                    mm = e.matmul(self.pmm[idx % 4][:, :],
                                  self.TB_L1[:, c * 128:(c + 1) * 128],
                                  src_buf[:, src_col(a):src_col(a) + 512],
                                  start=True, stop=True)
                    self.sem_inc(mm, "pe")

            def evac_body(e, parity, sem_name):
                for idx in range(parity, 32, 2):
                    c, a = idx // 16, idx % 16
                    e.wait_ge(self.sem["pe"], pe0 + idx + 1)
                    dst = P1[:, c * 8192 + a * 512:c * 8192 + (a + 1) * 512]
                    if parity == 0:
                        self.sem_inc(e.tensor_copy(dst, self.pmm[idx % 4][:, :]), sem_name)
                    else:
                        self.sem_inc(e.activation(dst, self.pmm[idx % 4][:, :], AF.Copy),
                                     sem_name)

            @blk.vector
            def _(e):
                evac_body(e, 0, "ev")

            @blk.scalar
            def _(e):
                evac_body(e, 1, "sc")

            @blk.gpsimd
            def _(e):
                for c in range(2):
                    for h in range(2):
                        hi = c * 16 + 8 * h + 8
                        e.wait_ge(self.sem["ev"], ev0 + (hi + 1) // 2)
                        e.wait_ge(self.sem["sc"], sc0 + hi // 2)
                        srcap = bass.AP(P1, c * 8192 + (8 * h) * 512,
                                        [[16384, 128], [512, 8], [1, 512]])
                        dst = bass.AP(self.hs1_t, c * (16 * 128 * DH) + (8 * h) * 65536,
                                      [[512, 128], [65536, 8], [1, 512]])
                        self.sem_inc(e.dma_start(dst, srcap), "gl", 16)
                e.wait_ge(self.sem["gl"], self.cnt["gl"])

            @blk.sync
            def _(e):
                if self.cur_chunk is not None:
                    st_, dh_ = self.cur_chunk
                    q0 = {"sA": self.cnt["sA"], "sB": self.cnt["sB"]}
                    qslot = lambda q: "sA" if q % 2 == 0 else "sB"
                    for q in range(2):
                        for c in range(2):
                            self.sem_inc(e.dma_start(
                                GGBQ(self, q, c),
                                self.gts[st_, c, dh_][:, 4 * q * 512:4 * q * 512 + 2048]),
                                qslot(q), 16)
                    self.gq0 = q0
                for c in range(2):
                    e.wait_ge(self.sem["gl"], gl_leg1_base + 32 * (c + 1))
                    srcap = bass.AP(self.hs1_t, c * (16 * 128 * DH),
                                    [[16 * DH, 128], [1, 16 * DH]])
                    self.sem_inc(e.dma_start(P2[:, c * 8192:(c + 1) * 8192], srcap), "ld", 16)
                e.wait_ge(self.sem["ld"], self.cnt["ld"])

        pe0, ev0, sc0 = self.cnt["pe"], self.cnt["ev"], self.cnt["sc"]
        with nc.Block() as blk:
            @blk.tensor
            def _(e):
                for idx in range(32):
                    cout, g = idx // 16, idx % 16
                    if idx >= 4:
                        evac_waits(e, ev0, sc0, idx - 4)
                    bank = self.pmm[idx % 4][:, :]
                    if cout == 0:
                        e.matmul(bank, self.TB_L2[:, (0 * 16 + g) * 128:(0 * 16 + g + 1) * 128],
                                 P2[:, 0 * 8192 + g * 512:0 * 8192 + (g + 1) * 512],
                                 start=True, stop=False)
                        mm = e.matmul(bank, self.TB_L2[:, (1 * 16 + g) * 128:(1 * 16 + g + 1) * 128],
                                      P2[:, 1 * 8192 + g * 512:1 * 8192 + (g + 1) * 512],
                                      start=False, stop=True)
                    else:
                        e.matmul(bank, self.TB_L2[:, (2 * 16 + g) * 128:(2 * 16 + g + 1) * 128],
                                 P2[:, 0 * 8192 + g * 512:0 * 8192 + (g + 1) * 512],
                                 start=True, stop=False)
                        mm = e.matmul(bank, self.TB_L2[:, (0 * 16 + g) * 128:(0 * 16 + g + 1) * 128],
                                      P2[:, 1 * 8192 + g * 512:1 * 8192 + (g + 1) * 512],
                                      start=False, stop=True)
                    self.sem_inc(mm, "pe")

            def evac_body2(e, parity, sem_name):
                for idx in range(parity, 32, 2):
                    cout, g = idx // 16, idx % 16
                    e.wait_ge(self.sem["pe"], pe0 + idx + 1)
                    dst = P1[:, cout * 8192 + g * 512:cout * 8192 + (g + 1) * 512]
                    if parity == 0:
                        self.sem_inc(e.tensor_copy(dst, self.pmm[idx % 4][:, :]), sem_name)
                    else:
                        self.sem_inc(e.activation(dst, self.pmm[idx % 4][:, :], AF.Copy),
                                     sem_name)

            @blk.vector
            def _(e):
                evac_body2(e, 0, "ev")

            @blk.scalar
            def _(e):
                evac_body2(e, 1, "sc")

    def conv_products(self, st, dh):
        nc = self.nc
        P1, P2 = self.P1, self.P2
        s0 = self.gq0
        qslot = lambda q: "sA" if q % 2 == 0 else "sB"
        self.prodev = {}
        self.prodgp = {}
        prodev, prodgp = self.prodev, self.prodgp
        ev0, gp0 = self.cnt["ev"], self.cnt["gp"]
        with nc.Block() as blk:
            @blk.sync
            def _(e):
                st_, dh_ = st, dh
                qslot = lambda q: "sA" if q % 2 == 0 else "sB"
                for q in (2, 3):
                    e.wait_ge(self.sem["ev"], ev0 + 4 * (q - 2) + 4)
                    e.wait_ge(self.sem["gp"], gp0 + 4 * (q - 2) + 4)
                    for c in range(2):
                        self.sem_inc(e.dma_start(
                            GGBQ(self, q, c),
                            self.gts[st_, c, dh_][:, 4 * q * 512:4 * q * 512 + 2048]),
                            qslot(q), 16)
                e.wait_ge(self.sem["sA"], self.cnt["sA"])
                e.wait_ge(self.sem["sB"], self.cnt["sB"])

            def tiles_of(g):
                q = g // 4
                base = (q % 2) * 4096 + (g % 4) * 512
                xre = P1[:, 0 * 8192 + g * 512:0 * 8192 + (g + 1) * 512]
                xim = P1[:, 1 * 8192 + g * 512:1 * 8192 + (g + 1) * 512]
                gre = self.GGB[:, base:base + 512]
                gim = self.GGB[:, 2048 + base:2048 + base + 512]
                zre = P2[:, 0 * 8192 + g * 512:0 * 8192 + (g + 1) * 512]
                zim = P2[:, 1 * 8192 + g * 512:1 * 8192 + (g + 1) * 512]
                return xre, xim, gre, gim, zre, zim

            @blk.vector
            def _(e):
                for g in range(16):
                    q = g // 4
                    if g % 4 == 0:
                        e.wait_ge(self.sem[qslot(q)], s0[qslot(q)] + 32 * (q // 2 + 1))
                    xre, xim, gre, gim, zre, zim = tiles_of(g)
                    e.tensor_mul(self.PR1[:, :], xre, gre)
                    i1 = e.tensor_mul(self.PR2[:, :], xim, gim)
                    e.wait_ge(self.sem["rwv"], self.sem_inc(i1, "rwv"))
                    self.sem_inc(e.tensor_sub(zre, self.PR1[:, :], self.PR2[:, :]), "ev")
                    e.wait_ge(self.sem["ev"], self.cnt["ev"])
                    if g % 4 == 3:
                        prodev[q] = self.cnt["ev"]

            @blk.gpsimd
            def _(e):
                for g in range(16):
                    q = g // 4
                    if g % 4 == 0:
                        e.wait_ge(self.sem[qslot(q)], s0[qslot(q)] + 32 * (q // 2 + 1))
                    xre, xim, gre, gim, zre, zim = tiles_of(g)
                    e.tensor_mul(self.PR3[:, :], xre, gim)
                    i3 = e.tensor_mul(self.PR4[:, :], xim, gre)
                    e.wait_ge(self.sem["rwg"], self.sem_inc(i3, "rwg"))
                    self.sem_inc(e.tensor_add(zim, self.PR3[:, :], self.PR4[:, :]), "gp")
                    e.wait_ge(self.sem["gp"], self.cnt["gp"])
                    if g % 4 == 3:
                        prodgp[q] = self.cnt["gp"]

    def conv_inv_half(self, dh, gate_cb):
        nc = self.nc
        P1, P2 = self.P1, self.P2

        pe0, ev0, sc0 = self.cnt["pe"], self.cnt["ev"], self.cnt["sc"]
        with nc.Block() as blk:
            @blk.tensor
            def _(e):
                for idx in range(32):
                    cout, g = idx // 16, idx % 16
                    if idx >= 4:
                        if (idx - 4) % 2 == 0:
                            e.wait_ge(self.sem["ev"], ev0 + (idx - 4) // 2 + 1)
                        else:
                            e.wait_ge(self.sem["sc"], sc0 + (idx - 4) // 2 + 1)
                    bank = self.pmm[idx % 4][:, :]
                    if cout == 0:
                        e.matmul(bank, self.TB_IL2[:, (0 * 16 + g) * 128:(0 * 16 + g + 1) * 128],
                                 P2[:, g * 512:(g + 1) * 512], start=True, stop=False)
                        mm = e.matmul(bank, self.TB_IL2[:, (2 * 16 + g) * 128:(2 * 16 + g + 1) * 128],
                                      P2[:, 8192 + g * 512:8192 + (g + 1) * 512],
                                      start=False, stop=True)
                    else:
                        e.matmul(bank, self.TB_IL2[:, (1 * 16 + g) * 128:(1 * 16 + g + 1) * 128],
                                 P2[:, g * 512:(g + 1) * 512], start=True, stop=False)
                        mm = e.matmul(bank, self.TB_IL2[:, (0 * 16 + g) * 128:(0 * 16 + g + 1) * 128],
                                      P2[:, 8192 + g * 512:8192 + (g + 1) * 512],
                                      start=False, stop=True)
                    self.sem_inc(mm, "pe")

            def evac_body(e, parity, sem_name):
                for idx in range(parity, 32, 2):
                    cout, g = idx // 16, idx % 16
                    e.wait_ge(self.sem["pe"], pe0 + idx + 1)
                    dst = P1[:, cout * 8192 + g * 512:cout * 8192 + (g + 1) * 512]
                    if parity == 0:
                        self.sem_inc(e.tensor_copy(dst, self.pmm[idx % 4][:, :]), sem_name)
                    else:
                        self.sem_inc(e.activation(dst, self.pmm[idx % 4][:, :], AF.Copy),
                                     sem_name)

            @blk.vector
            def _(e):
                evac_body(e, 0, "ev")

            def leg1(e, c):
                for ap_ in range(16):
                    srcap = bass.AP(P1, (ap_ * 8) * 16384 + c * 8192,
                                    [[16384, 8], [512, 16], [1, 512]])
                    dst = bass.AP(self.hs2_t, c * (128 * 16 * DH) + ap_ * DH,
                                  [[16 * 16 * DH, 8], [16 * DH, 16], [1, DH]])
                    self.sem_inc(e.dma_start(dst, srcap), "gl" if c == 0 else "st", 16)

            @blk.gpsimd
            def _(e):
                e.wait_ge(self.sem["ev"], ev0 + 16)
                e.wait_ge(self.sem["sc"], sc0 + 16)
                leg1(e, 0)
                e.wait_ge(self.sem["gl"], self.cnt["gl"])

            @blk.scalar
            def _(e2):
                evac_body(e2, 1, "sc")
                e2.wait_ge(self.sem["ev"], ev0 + 16)
                leg1(e2, 1)
                e2.wait_ge(self.sem["st"], self.cnt["st"])

            @blk.sync
            def _(e):
                e.wait_ge(self.sem["gl"], self.cnt["gl"])
                e.wait_ge(self.sem["st"], self.cnt["st"])
                for c in range(2):
                    srcap = bass.AP(self.hs2_t, c * (128 * 16 * DH),
                                    [[16 * DH, 128], [1, 16 * DH]])
                    self.sem_inc(e.dma_start(P2[:, c * 8192:(c + 1) * 8192], srcap), "ld", 16)
                e.wait_ge(self.sem["ld"], self.cnt["ld"])

        self.gate_ev_base = self.cnt["ev"]
        pe0 = self.cnt["pe"]
        with nc.Block() as blk:
            @blk.tensor
            def _(e):
                for ap_ in range(16):
                    if ap_ >= 2:
                        e.wait_ge(self.sem["ev"], self.gate_ev_base + (ap_ - 2) + 1)
                    bank = self.pmm[4 + ap_ % 2][:, :]
                    e.matmul(bank, self.TB_IL1[:, 0:128],
                             P2[:, ap_ * 512:(ap_ + 1) * 512], start=True, stop=False)
                    mm = e.matmul(bank, self.TB_IL1[:, 128:256],
                                  P2[:, 8192 + ap_ * 512:8192 + (ap_ + 1) * 512],
                                  start=False, stop=True)
                    self.sem_inc(mm, "pe")

            @blk.sync
            def _(e):
                self.gate_sync(e)

            @blk.vector
            def _(e):
                for ap_ in range(16):
                    e.wait_ge(self.sem["pe"], pe0 + ap_ + 1)
                    gate_cb(e, ap_, self.pmm[4 + ap_ % 2][:, :])

    def phase_filter_prep(self):
        nc = self.nc
        with contextlib.ExitStack() as pes:
            psb = lambda name, shape, dt: pes.enter_context(nc.sbuf_tensor(name, shape, dt))
            FV = psb("FV", [128, 16 * D], BF16)
            self.P1 = psb("P1f", [128, 16384], BF16)
            self.P2 = psb("P2f", [128, 16384], BF16)

            for st in range(2):
                gl0 = self.cnt["gl"]
                with nc.Block() as blk:
                    @blk.gpsimd
                    def _(e):
                        for a in range(NPH):
                            self.sem_inc(e.dma_start(FV[:, a * D:(a + 1) * D],
                                                     self.ft_in[st, a::16, :]), "gl", 16)
                for dh in range(2):
                    self.cur_chunk = None
                    self.conv_fwd_half(FV, lambda a, dh=dh: a * D + dh * DH, dh,
                                       pre_tensor_waits=[("gl", gl0 + 16 * NPH)])
                    with nc.Block() as blk:
                        @blk.sync
                        def _(e):
                            for c in range(2):
                                self.sem_inc(e.dma_start(self.gts[st, c, dh],
                                                         self.P1[:, c * 8192:(c + 1) * 8192]),
                                             "st", 16)
                            e.wait_ge(self.sem["st"], self.cnt["st"])
            del self.P1, self.P2

    def ln_transpose(self, tile_specs, xf_dst):
        nc = self.nc
        n = len(tile_specs)
        pe0, ev0, sc0, te0 = (self.cnt[k] for k in ("pe", "ev", "sc", "te"))
        s0 = {"sA": self.cnt["sA"], "sB": self.cnt["sB"]}
        slot_of = lambda i: "sA" if i % 2 == 0 else "sB"
        slot_w = lambda i: s0[slot_of(i)] + 16 * (i // 2 + 1)
        with nc.Block() as blk:
            @blk.sync
            def _(e):
                for i in range(n):
                    if i >= 2:
                        e.wait_ge(self.sem["ev"], ev0 + 3 * (i - 2) + 3)
                        e.wait_ge(self.sem["sc"], sc0 + 5 * (i - 2) + 1)
                    self.sem_inc(e.dma_start(self.XT[:, (i % 2) * D:(i % 2 + 1) * D],
                                             tile_specs[i][0]), slot_of(i), 16)
                e.wait_ge(self.sem["sA"], self.cnt["sA"])
                e.wait_ge(self.sem["sB"], self.cnt["sB"])

            @blk.tensor
            def _(e):
                e.wait_ge(self.sem["gl"], self.gl_ident)
                for i in range(n):
                    bix = i % 2
                    e.wait_ge(self.sem["ev"], ev0 + 3 * i + 3)
                    if i >= 1:
                        e.wait_ge(self.sem["te"], te0 + i)
                    for k in range(8):
                        mm = e.transpose(self.ptr[:, k * 128:(k + 1) * 128],
                                         self.LNT[:, bix * D + k * 128:bix * D + (k + 1) * 128],
                                         self.IDENT[:, :])
                        if k == 7:
                            self.sem_inc(mm, "pe")

            @blk.vector
            def _(e):
                for i in range(n):
                    bix = i % 2
                    xt = self.XT[:, bix * D:(bix + 1) * D]
                    c = lambda role: self.STT[:, bix * 8 + role:bix * 8 + role + 1]
                    e.wait_ge(self.sem[slot_of(i)], slot_w(i))
                    if i >= 2:
                        e.wait_ge(self.sem["sc"], sc0 + 5 * (i - 2) + 3)
                    self.sem_inc(e.tensor_reduce(c(0), xt, AX.X, ALU.add), "ev")
                    e.wait_ge(self.sem["sc"], sc0 + 5 * i + 3)
                    self.sem_inc(e.scalar_tensor_tensor(c(5), c(1), 1.0 / D, c(3),
                                                        ALU.mult, ALU.subtract), "ev")
                    e.wait_ge(self.sem["sc"], sc0 + 5 * i + 5)
                    if i >= 2:
                        e.wait_ge(self.sem["pe"], pe0 + (i - 2) + 1)
                    self.sem_inc(e.tensor_scalar(self.LNT[:, bix * D:(bix + 1) * D], xt,
                                                 c(2), c(7), ALU.subtract, ALU.mult), "ev")
                    if i >= 1:
                        e.wait_ge(self.sem["pe"], pe0 + i)
                        self.sem_inc(e.tensor_copy(
                            xf_dst[:, tile_specs[i - 1][1]:tile_specs[i - 1][1] + 1024],
                            self.ptr[:, :]), "te")
                e.wait_ge(self.sem["pe"], pe0 + n)
                self.sem_inc(e.tensor_copy(
                    xf_dst[:, tile_specs[n - 1][1]:tile_specs[n - 1][1] + 1024],
                    self.ptr[:, :]), "te")

            @blk.scalar
            def _(e):
                for i in range(n):
                    bix = i % 2
                    xt = self.XT[:, bix * D:(bix + 1) * D]
                    c = lambda role: self.STT[:, bix * 8 + role:bix * 8 + role + 1]
                    e.wait_ge(self.sem[slot_of(i)], slot_w(i))
                    self.sem_inc(e.activation(self.SQ[:, :], xt, AF.Square,
                                              accum_out=c(1)), "sc")
                    e.wait_ge(self.sem["ev"], ev0 + 3 * i + 1)
                    self.sem_inc(e.activation(c(2), c(0), AF.Copy, scale=1.0 / D), "sc")
                    self.sem_inc(e.activation(c(3), c(0), AF.Square, scale=1.0 / D), "sc")
                    e.wait_ge(self.sem["ev"], ev0 + 3 * i + 2)
                    self.sem_inc(e.activation(c(6), c(5), AF.Ln, bias=self.EPS[:, :]), "sc")
                    e.wait_ge(self.sem["sc"], self.cnt["sc"])
                    self.sem_inc(e.activation(c(7), c(6), AF.Exp, scale=-0.5), "sc")

    def phase_batch(self, b, es):
        nc = self.nc
        with contextlib.ExitStack() as bes:
            bsb = lambda name, shape, dt: bes.enter_context(nc.sbuf_tensor(name, shape, dt))
            VA = bsb(f"VA{b}", [128, 16 * D], BF16)

            with contextlib.ExitStack() as ses:
                ssb = lambda name, shape, dt: ses.enter_context(nc.sbuf_tensor(name, shape, dt))
                XF = ssb(f"XF{b}", [128, 16 * D], BF16)
                WW = ssb(f"WW{b}", [128, 8 * 3 * D], BF16)
                PBREP = ssb(f"PBREP{b}", [128, 3 * D], F32)
                GS = ssb(f"GS{b}", [128, 4 * D], BF16)

                gl0 = self.cnt["gl"]
                with nc.Block() as blk:
                    @blk.gpsimd
                    def _(e):
                        for ec in range(8):
                            self.sem_inc(e.dma_start(WW[:, ec * 3072:(ec + 1) * 3072],
                                                     self.pw_in[ec * 128:(ec + 1) * 128, :]),
                                         "gl", 16)

                    @blk.sync
                    def _(e):
                        self.sem_inc(e.dma_start(
                            PBREP[:, :], self.pb_in[None, :].broadcast_to([128, 3 * D])),
                            "ld", 16)

                self.ln_transpose(
                    [(self.x_in[b, a::16, :], a * D) for a in range(NPH)], XF)
                self.maybe_dump(f"xf{b}", XF[:, :], [128, 16 * D], BF16)

                pe0, ev0 = self.cnt["pe"], self.cnt["ev"]
                t0 = {"tA": self.cnt["tA"], "tB": self.cnt["tB"]}
                tslot = lambda a: "tA" if a % 2 == 0 else "tB"
                ld_pb = self.cnt["ld"]
                gl_ww = gl0 + 16 * 8
                with nc.Block() as blk:
                    @blk.tensor
                    def _(e):
                        e.wait_ge(self.sem["gl"], gl_ww)
                        for a in range(NPH):
                            for h in range(2):
                                gidx = a * 2 + h
                                if gidx >= 2:
                                    e.wait_ge(self.sem["ev"], ev0 + 3 * (gidx - 1))
                                for ec in range(8):
                                    for j in range(3):
                                        jj = h * 3 + j
                                        mm = e.matmul(
                                            self.pmm[jj][:, :],
                                            XF[:, a * D + ec * 128:a * D + (ec + 1) * 128],
                                            WW[:, ec * 3072 + jj * 512:ec * 3072 + (jj + 1) * 512],
                                            start=(ec == 0), stop=(ec == 7))
                                self.sem_inc(mm, "pe")

                    @blk.vector
                    def _(e):
                        e.wait_ge(self.sem["ld"], ld_pb)
                        for a in range(NPH):
                            for h in range(2):
                                gidx = a * 2 + h
                                e.wait_ge(self.sem["pe"], pe0 + gidx + 1)
                                if gidx >= 4:
                                    e.wait_ge(self.sem[tslot(a)],
                                              t0[tslot(a)] + 32 * ((a - 2) // 2 + 1))
                                for j in range(3):
                                    jj = h * 3 + j
                                    src = self.pmm[jj][:, :]
                                    brp = PBREP[:, jj * 512:(jj + 1) * 512]
                                    if jj < 2:
                                        dst = VA[:, a * D + jj * 512:a * D + (jj + 1) * 512]
                                    elif jj < 4:
                                        dst = GS[:, (a % 2) * D + (jj - 2) * 512:
                                                 (a % 2) * D + (jj - 1) * 512]
                                    else:
                                        dst = GS[:, 2048 + (a % 2) * D + (jj - 4) * 512:
                                                 2048 + (a % 2) * D + (jj - 3) * 512]
                                    self.sem_inc(e.tensor_tensor(dst, src, brp, ALU.add), "ev")

                    @blk.sync
                    def _(e):
                        for a in range(NPH):
                            e.wait_ge(self.sem["ev"], ev0 + 6 * (a + 1))
                            self.sem_inc(e.dma_start(self.g1sp[b, a],
                                                     GS[:, (a % 2) * D:(a % 2 + 1) * D]),
                                         tslot(a), 16)
                            self.sem_inc(e.dma_start(self.g2sp[b, a],
                                                     GS[:, 2048 + (a % 2) * D:2048 + (a % 2 + 1) * D]),
                                         tslot(a), 16)
                        e.wait_ge(self.sem["tA"], self.cnt["tA"])
                        e.wait_ge(self.sem["tB"], self.cnt["tB"])
                self.maybe_dump(f"va{b}", VA[:, :], [128, 16 * D], BF16)

            if self.batch_stop == "proj":
                return
            with contextlib.ExitStack() as ces:
                csb = lambda name, shape, dt: ces.enter_context(nc.sbuf_tensor(name, shape, dt))
                VB = csb(f"VB{b}", [128, 16 * D], BF16)
                self.P1 = csb(f"P1b{b}", [128, 16384], BF16)
                self.P2 = csb(f"P2b{b}", [128, 16384], BF16)
                self.GGB = csb(f"GGB{b}", [128, 8192], BF16)
                self.PR1 = csb(f"PR1b{b}", [128, 512], BF16)
                self.PR2 = csb(f"PR2b{b}", [128, 512], BF16)
                self.PR3 = csb(f"PR3b{b}", [128, 512], BF16)
                self.PR4 = csb(f"PR4b{b}", [128, 512], BF16)
                GT = csb(f"GT{b}", [128, 2048], BF16)
                TMPG = csb(f"TMPG{b}", [128, 512], F32)
                X1H = csb(f"X1H{b}", [128, 2048], F32)

                nstage = 1 if self.batch_stop == "s1" else 2
                for st in range(nstage):
                    src = VA if st == 0 else VB
                    gsp = self.g1sp if st == 0 else self.g2sp
                    for dh in range(2):
                        self.cur_chunk = (st, dh)
                        self.conv_fwd_half(src, lambda a, dh=dh: a * D + dh * DH, dh)
                        self.conv_products(st, dh)

                        gate_state = {}

                        gslot = lambda grp: "sA" if grp % 2 == 0 else "sB"
                        gtslot = lambda grp: "tA" if grp % 2 == 0 else "tB"
                        per = 16 if st == 0 else 32
                        g1t = self.g1sp.tensor if hasattr(self.g1sp, "tensor") else None

                        def gate_sync(e, b=b, st=st, dh=dh, gsp=gsp):
                            gate_state["s0"] = {"sA": self.cnt["sA"], "sB": self.cnt["sB"]}
                            gate_state["t0"] = {"tA": self.cnt["tA"], "tB": self.cnt["tB"]}
                            evb = self.gate_ev_base
                            gspt = gsp.tensor
                            xt_t = self.x_in.tensor

                            def store_pair(k):
                                dst = bass.AP(self.x1sp.tensor,
                                              (b * 16 + 2 * k) * 131072 + dh * DH,
                                              [[1024, 128], [131072, 2], [1, 512]])
                                srcap = bass.AP(X1H, (k % 2) * 1024,
                                                [[2048, 128], [512, 2], [1, 512]])
                                self.sem_inc(e.dma_start(dst, srcap), gtslot(k), 16)

                            for grp in range(8):
                                if grp >= 2:
                                    e.wait_ge(self.sem["ev"], evb + 2 * (grp - 2) + 2)
                                    if st == 1:
                                        store_pair(grp - 2)
                                gt_src = bass.AP(gspt,
                                                 (b * 16 + 2 * grp) * 131072 + dh * DH,
                                                 [[1024, 128], [131072, 2], [1, 512]])
                                self.sem_inc(e.dma_start(
                                    GT[:, (grp % 2) * 1024:(grp % 2 + 1) * 1024]
                                    .rearrange("p (a d) -> p a d", a=2), gt_src),
                                    gslot(grp), 16)
                                if st == 1:
                                    x_src = bass.AP(xt_t,
                                                    (b * 2048 + 2 * grp) * 1024 + dh * DH,
                                                    [[16 * 1024, 128], [1024, 2], [1, 512]])
                                    self.sem_inc(e.dma_start(
                                        self.XT[:, (grp % 2) * 1024:(grp % 2 + 1) * 1024]
                                        .rearrange("p (a d) -> p a d", a=2), x_src),
                                        gslot(grp), 16)
                            if st == 1:
                                for k in (6, 7):
                                    e.wait_ge(self.sem["ev"], evb + 2 * k + 2)
                                    store_pair(k)
                                e.wait_ge(self.sem["tA"], self.cnt["tA"])
                                e.wait_ge(self.sem["tB"], self.cnt["tB"])
                            e.wait_ge(self.sem["sA"], self.cnt["sA"])
                            e.wait_ge(self.sem["sB"], self.cnt["sB"])

                        def gt_slice(ap_):
                            grp = ap_ // 2
                            base = (grp % 2) * 1024 + (ap_ % 2) * 512
                            return GT[:, base:base + 512]

                        def xt_slice(ap_):
                            grp = ap_ // 2
                            base = (grp % 2) * 1024 + (ap_ % 2) * 512
                            return self.XT[:, base:base + 512]

                        def x1h_slice(ap_):
                            grp = ap_ // 2
                            base = (grp % 2) * 1024 + (ap_ % 2) * 512
                            return X1H[:, base:base + 512]

                        if st == 0:
                            def gate_cb(e, ap_, psum, dh=dh):
                                grp = ap_ // 2
                                sl = gslot(grp)
                                e.wait_ge(self.sem[sl],
                                          gate_state["s0"][sl] + per * (grp // 2 + 1))
                                self.sem_inc(e.tensor_tensor(
                                    VB[:, ap_ * D + dh * DH:ap_ * D + (dh + 1) * DH],
                                    psum, gt_slice(ap_), ALU.mult), "ev")
                        else:
                            def gate_cb(e, ap_, psum, dh=dh):
                                grp = ap_ // 2
                                sl = gslot(grp)
                                e.wait_ge(self.sem[sl],
                                          gate_state["s0"][sl] + per * (grp // 2 + 1))
                                if grp >= 2:
                                    tl = gtslot(grp)
                                    e.wait_ge(self.sem[tl],
                                              gate_state["t0"][tl] + 16 * ((grp - 2) // 2 + 1))
                                i1 = e.tensor_tensor(TMPG[:, :], psum, gt_slice(ap_),
                                                     ALU.mult)
                                e.wait_ge(self.sem["rwv"], self.sem_inc(i1, "rwv"))
                                self.sem_inc(e.tensor_tensor(
                                    x1h_slice(ap_), TMPG[:, :], xt_slice(ap_),
                                    ALU.add), "ev")
                                e.wait_ge(self.sem["ev"], self.cnt["ev"])

                        self.gate_sync = gate_sync
                        self.conv_inv_half(dh, gate_cb)

                self.maybe_dump(f"vb{b}", VB[:, :], [128, 16 * D], BF16)
                del self.P1, self.P2

            cnt = dict(self.cnt)
            with nc.Block() as blk:
                @blk.sync
                def _(e):
                    for n in ("ld", "st", "sA", "sB", "tA", "tB"):
                        if cnt[n]:
                            e.wait_ge(self.sem[n], cnt[n])

                @blk.gpsimd
                def _(e):
                    e.wait_ge(self.sem["gl"], cnt["gl"])

    def phase_ffn(self, es):
        nc = self.nc
        with contextlib.ExitStack() as fes:
            fsb = lambda name, shape, dt: fes.enter_context(nc.sbuf_tensor(name, shape, dt))
            W1S = fsb("W1S", [128, 8 * 4096], BF16)
            W2S = fsb("W2S", [128, 32 * D], BF16)
            HT = fsb("HT", [128, 32 * 512], BF16)
            LN2FM = fsb("LN2FM", [128, 4 * D], BF16)
            B1C = fsb("B1C", [128, 32], F32)
            B2R = fsb("B2R", [128, D], F32)
            X1T = fsb("X1T", [128, 2 * D], F32)
            OTS = fsb("OTS", [128, 2048], F32)

            gl0 = self.cnt["gl"]
            with nc.Block() as blk:
                @blk.gpsimd
                def _(e):
                    for ec in range(8):
                        self.sem_inc(e.dma_start(W1S[:, ec * 4096:(ec + 1) * 4096],
                                                 self.w1_in[ec * 128:(ec + 1) * 128, :]),
                                     "gl", 16)
                    for hc in range(32):
                        self.sem_inc(e.dma_start(W2S[:, hc * D:(hc + 1) * D],
                                                 self.w2_in[hc * 128:(hc + 1) * 128, :]),
                                     "gl", 16)

                @blk.sync
                def _(e):
                    self.sem_inc(e.dma_start(
                        B1C[:, :], self.b1_in.rearrange("(c p) -> p c", p=128),
                        allow_slow_non_contiguous=True), "ld", 16)
                    self.sem_inc(e.dma_start(
                        B2R[:, :], self.b2_in[None, :].broadcast_to([128, D])), "ld", 16)
                    e.wait_ge(self.sem["ld"], self.cnt["ld"])
            gl_w = self.cnt["gl"]

            if self.ffn_stop == "load":
                return
            for b in range(NB):
                for ch in range(4):
                    tiles = [4 * ch + i for i in range(4)]
                    self.ln_transpose(
                        [(self.x1sp[b, t], i * D) for i, t in enumerate(tiles)], LN2FM)
                    if self.ffn_stop == "ln":
                        continue

                    pe0, sc0 = self.cnt["pe"], self.cnt["sc"]
                    with nc.Block() as blk:
                        @blk.tensor
                        def _(e):
                            e.wait_ge(self.sem["gl"], gl_w)
                            for hc in range(32):
                                if hc >= 2:
                                    e.wait_ge(self.sem["sc"], sc0 + hc - 1)
                                for ec in range(8):
                                    rhs = bass.AP(LN2FM, ec * 128,
                                                  [[4 * D, 128], [D, 4], [1, 128]])
                                    mm = e.matmul(self.pmm[hc % 2][:, :],
                                                  W1S[:, ec * 4096 + hc * 128:
                                                      ec * 4096 + (hc + 1) * 128],
                                                  rhs, start=(ec == 0), stop=(ec == 7))
                                self.sem_inc(mm, "pe")

                        @blk.scalar
                        def _(e):
                            for hc in range(32):
                                e.wait_ge(self.sem["pe"], pe0 + hc + 1)
                                self.sem_inc(e.activation(
                                    HT[:, hc * 512:(hc + 1) * 512],
                                    self.pmm[hc % 2][:, :], AF.Gelu,
                                    bias=B1C[:, hc:hc + 1]), "sc")

                    if self.ffn_stop == "ffn1":
                        continue
                    if self.ffn_stop == "ffn2first" and (b, ch) != (0, 0):
                        continue
                    pe0, ev0 = (self.cnt[k] for k in ("pe", "ev"))
                    s0f = {"sA": self.cnt["sA"], "sB": self.cnt["sB"]}
                    t0f = {"tA": self.cnt["tA"], "tB": self.cnt["tB"]}
                    fslot = lambda i: "sA" if i % 2 == 0 else "sB"
                    ftslot = lambda gi: "tA" if gi % 2 == 0 else "tB"
                    with nc.Block() as blk:
                        @blk.sync
                        def _(e):
                            for i in (0, 1):
                                self.sem_inc(e.dma_start(
                                    self.X1T_dst(X1T, i), self.x1sp[b, tiles[i]]),
                                    fslot(i), 16)
                            for gi in range(8):
                                i, j = gi // 2, gi % 2
                                e.wait_ge(self.sem["ev"], ev0 + gi + 1)
                                slot = (gi % 2) * 1024 + 512
                                self.sem_inc(e.dma_start(
                                    self.out[b, tiles[i]::16, j * 512:(j + 1) * 512],
                                    OTS[:, slot:slot + 512]), ftslot(gi), 16)
                                if gi in (1, 3):
                                    ii = 2 + (gi - 1) // 2
                                    self.sem_inc(e.dma_start(
                                        self.X1T_dst(X1T, ii), self.x1sp[b, tiles[ii]]),
                                        fslot(ii), 16)
                            e.wait_ge(self.sem["tA"], self.cnt["tA"])
                            e.wait_ge(self.sem["tB"], self.cnt["tB"])
                            e.wait_ge(self.sem["sA"], self.cnt["sA"])
                            e.wait_ge(self.sem["sB"], self.cnt["sB"])

                        @blk.tensor
                        def _(e):
                            for i in range(4):
                                if i >= 2:
                                    e.wait_ge(self.sem["ev"], ev0 + 2 * (i - 2) + 2)
                                b0 = self.pmm[2 + 2 * (i % 2)][:, :]
                                b1 = self.pmm[3 + 2 * (i % 2)][:, :]
                                for hc in range(32):
                                    lhsT = HT[:, hc * 512 + i * 128:hc * 512 + (i + 1) * 128]
                                    e.matmul(b0, lhsT, W2S[:, hc * D:hc * D + 512],
                                             start=(hc == 0), stop=(hc == 31))
                                    mm = e.matmul(b1, lhsT, W2S[:, hc * D + 512:(hc + 1) * D],
                                                  start=(hc == 0), stop=(hc == 31))
                                self.sem_inc(mm, "pe")

                        @blk.vector
                        def _(e):
                            for gi in range(8):
                                i, j = gi // 2, gi % 2
                                e.wait_ge(self.sem["pe"], pe0 + i + 1)
                                e.wait_ge(self.sem[fslot(i)], s0f[fslot(i)] + 16 * (i // 2 + 1))
                                if gi >= 2:
                                    tl = ftslot(gi)
                                    e.wait_ge(self.sem[tl], t0f[tl] + 16 * ((gi - 2) // 2 + 1))
                                slot = (gi % 2) * 1024
                                i1 = e.tensor_tensor(OTS[:, slot:slot + 512],
                                                     self.pmm[2 + 2 * (i % 2) + j][:, :],
                                                     X1T[:, (i % 2) * D + j * 512:
                                                         (i % 2) * D + (j + 1) * 512],
                                                     ALU.add)
                                e.wait_ge(self.sem["rwv"], self.sem_inc(i1, "rwv"))
                                self.sem_inc(e.tensor_tensor(
                                    OTS[:, slot + 512:slot + 1024],
                                    OTS[:, slot:slot + 512],
                                    B2R[:, j * 512:(j + 1) * 512], ALU.add), "ev")

    def X1T_dst(self, X1T, i):
        return X1T[:, (i % 2) * D:(i % 2 + 1) * D]


_CACHE = {}


def _build():
    if "nc" not in _CACHE:
        b = Builder()
        _CACHE["nc"] = b.build()
    return _CACHE["nc"]


def kernel(L0_attn_x, L0_attn_proj_w, L0_attn_proj_b, L0_attn_filter_time,
           L0_ffn_w1, L0_ffn_b1, L0_ffn_w2, L0_ffn_b2):
    nc = _build()
    t_l1, t_l2, t_il2, t_il1 = host_tables()
    ident = np.eye(128, dtype=np.float32)
    x = np.ascontiguousarray(np.asarray(L0_attn_x, dtype=np.float32))
    ft = np.ascontiguousarray(np.asarray(L0_attn_filter_time, dtype=np.float32)
                              .reshape(2, L, D))
    shared = {
        "pw": np.ascontiguousarray(L0_attn_proj_w, dtype=np.float32),
        "pb": np.ascontiguousarray(L0_attn_proj_b, dtype=np.float32),
        "ft": ft,
        "w1": np.ascontiguousarray(L0_ffn_w1, dtype=np.float32),
        "b1": np.ascontiguousarray(L0_ffn_b1, dtype=np.float32),
        "w2": np.ascontiguousarray(L0_ffn_w2, dtype=np.float32),
        "b2": np.ascontiguousarray(L0_ffn_b2, dtype=np.float32),
        "tl1": t_l1, "tl2": t_l2, "til2": t_il2, "til1": t_il1,
        "idm": ident,
    }
    in_maps = []
    for c in range(NCORES):
        m = dict(shared)
        m["x"] = np.ascontiguousarray(x[c * NB:(c + 1) * NB])
        in_maps.append(m)
    res = run_bass_kernel_spmd(nc, in_maps, core_ids=list(range(NCORES)))
    return np.concatenate([r["out"] for r in res.results], axis=0)


# version 07
# speedup vs baseline: 1.0658x; 1.0047x over previous
import contextlib
import math

import numpy as np

import concourse.bass as bass
import concourse.mybir as mybir
from concourse.bass_utils import run_bass_kernel_spmd

F32 = mybir.dt.float32
BF16 = mybir.dt.bfloat16
AF = mybir.ActivationFunctionType
ALU = mybir.AluOpType
AX = mybir.AxisListType

P4 = 4096
NPH = 16
L = 2048
D = 1024
DH = 512
NB = 2
NCORES = 8


def _red(m):
    r = np.mod(m, 2 * P4)
    r = np.where(r > P4, r - 2 * P4, r)
    return r.astype(np.float32)


def GGBQ(self, q, c):
    base = (q % 2) * 4096 + c * 2048
    return self.GGB[:, base:base + 2048]


def host_tables():
    COS = P4 // 2
    r = np.arange(128)
    kap_of_r = 8 * (r % 16) + r // 16
    b = np.arange(128)
    base1 = 16 * np.outer(b, 2 * kap_of_r + 1)
    t_l1 = np.stack([_red(base1 + COS), _red(-base1)])
    m = np.arange(128)
    basei = 16 * np.outer(2 * kap_of_r + 1, m)
    t_il1 = np.stack([_red(basei + COS), _red(-basei)])
    a = np.arange(NPH)
    t_l2 = np.zeros((3, 16, 128, 128), np.float32)
    t_il2 = np.zeros((3, 16, 128, 128), np.float32)
    for g in range(16):
        for gam in range(8):
            kpp = 8 * g + gam
            rows = a * 8 + gam
            th = a[:, None] * (2 * kpp + 1) + 512 * np.outer(a, a)
            t_l2[0, g, rows[:, None], rows[None, :]] = _red(th + COS)
            t_l2[1, g, rows[:, None], rows[None, :]] = _red(th)
            t_l2[2, g, rows[:, None], rows[None, :]] = _red(-th)
            thi = a[None, :] * (2 * kpp + 1) + 512 * np.outer(a, a)
            t_il2[0, g, rows[:, None], rows[None, :]] = _red(thi + COS)
            t_il2[1, g, rows[:, None], rows[None, :]] = _red(thi)
            t_il2[2, g, rows[:, None], rows[None, :]] = _red(-thi)
    return t_l1, t_l2, t_il2, t_il1


class Builder:
    def __init__(self, debug_dumps=(), stop_after="full", batch_stop=None, ffn_stop=None):
        self.nc = bass.Bass("TRN2", target_bir_lowering=False)
        self.debug_dumps = set(debug_dumps)
        self.stop_after = stop_after
        self.batch_stop = batch_stop
        self.ffn_stop = ffn_stop
        self.prodev = {}
        self.prodgp = {}
        self.dbg = {}

    def sem_inc(self, inst, name, k=1):
        inst.then_inc(self.sem[name], k)
        self.cnt[name] += k
        return self.cnt[name]

    def fence(self, e, inst, name):
        self.sem_inc(inst, name)
        e.wait_ge(self.sem[name], self.cnt[name])

    def maybe_dump(self, tag, src_ap, shape, dtype):
        if tag not in self.debug_dumps:
            return
        t = self.nc.dram_tensor("dbg_" + tag, shape, dtype, kind="ExternalOutput").ap()
        self.dbg[tag] = t
        with self.nc.Block() as blk:
            @blk.sync
            def _(e):
                self.sem_inc(e.dma_start(t, src_ap), "st", 16)
                e.wait_ge(self.sem["st"], self.cnt["st"])

    def build(self):
        nc = self.nc
        self.x_in = nc.dram_tensor("x", [NB, L, D], F32, kind="ExternalInput").ap()
        self.pw_in = nc.dram_tensor("pw", [D, 3 * D], F32, kind="ExternalInput").ap()
        self.pb_in = nc.dram_tensor("pb", [3 * D], F32, kind="ExternalInput").ap()
        self.ft_in = nc.dram_tensor("ft", [2, L, D], F32, kind="ExternalInput").ap()
        self.w1_in = nc.dram_tensor("w1", [D, 4 * D], F32, kind="ExternalInput").ap()
        self.b1_in = nc.dram_tensor("b1", [4 * D], F32, kind="ExternalInput").ap()
        self.w2_in = nc.dram_tensor("w2", [4 * D, D], F32, kind="ExternalInput").ap()
        self.b2_in = nc.dram_tensor("b2", [D], F32, kind="ExternalInput").ap()
        self.tl1_in = nc.dram_tensor("tl1", [2, 128, 128], F32, kind="ExternalInput").ap()
        self.tl2_in = nc.dram_tensor("tl2", [3, 16, 128, 128], F32, kind="ExternalInput").ap()
        self.til2_in = nc.dram_tensor("til2", [3, 16, 128, 128], F32, kind="ExternalInput").ap()
        self.til1_in = nc.dram_tensor("til1", [2, 128, 128], F32, kind="ExternalInput").ap()
        self.id_in = nc.dram_tensor("idm", [128, 128], F32, kind="ExternalInput").ap()
        self.out = nc.dram_tensor("out", [NB, L, D], F32, kind="ExternalOutput").ap()

        self.g1sp = nc.dram_tensor("g1sp", [NB, 16, 128, D], BF16).ap()
        self.g2sp = nc.dram_tensor("g2sp", [NB, 16, 128, D], BF16).ap()
        self.x1sp = nc.dram_tensor("x1sp", [NB, 16, 128, D], F32).ap()
        self.gts = nc.dram_tensor("gts", [2, 2, 2, 128, 16 * DH], BF16).ap()
        self.hs1_t = nc.dram_tensor("hs1", [2, 16, 128, DH], BF16)
        self.hs2_t = nc.dram_tensor("hs2", [2, 128, 16, DH], BF16)
        self.hs1 = self.hs1_t.ap()

        es = contextlib.ExitStack()
        with es:
            sem_names = ["ld", "gl", "st", "pe", "ev", "sc", "te", "sA", "sB", "tA", "tB", "rwv", "rws", "gp", "rwg"]
            self.sem = {n: es.enter_context(nc.semaphore(n)) for n in sem_names}
            self.cnt = {n: 0 for n in sem_names}
            sb = lambda name, shape, dt: es.enter_context(nc.sbuf_tensor(name, shape, dt))

            self.IDENT = sb("IDENT", [128, 128], BF16)
            self.EPS = sb("EPS", [128, 1], F32)
            self.STT = sb("STT", [128, 16], F32)
            self.XT = sb("XT", [128, 2 * D], F32)
            self.SQ = sb("SQ", [128, D], F32)
            self.LNT = sb("LNT", [128, 2 * D], BF16)
            self.pmm = [es.enter_context(nc.psum_tensor(f"pm{i}", [128, 512], F32))
                        for i in range(6)]
            self.ptr = es.enter_context(nc.psum_tensor("ptr", [128, 1024], BF16))

            with nc.Block() as blk:
                @blk.vector
                def _(e):
                    e.memset(self.EPS[:, :], 1e-5)

                @blk.gpsimd
                def _(e):
                    self.sem_inc(e.dma_start(self.IDENT[:, :], self.id_in[:, :]), "gl", 16)
            self.gl_ident = self.cnt["gl"]

            with contextlib.ExitStack() as tes:
                tsb = lambda name, shape, dt: tes.enter_context(nc.sbuf_tensor(name, shape, dt))
                self.TB_L1 = tsb("TB_L1", [128, 256], BF16)
                self.TB_IL1 = tsb("TB_IL1", [128, 256], BF16)
                self.TB_L2 = tsb("TB_L2", [128, 48 * 128], BF16)
                self.TB_IL2 = tsb("TB_IL2", [128, 48 * 128], BF16)
                self.phase_tables(tsb)
                if self.stop_after != "tables":
                    if self.stop_after != "notables_filter":
                        self.phase_filter_prep()
                    if self.stop_after not in ("filter",):
                        for b in range(NB):
                            self.phase_batch(b, es)
                            if self.stop_after == "b0":
                                break
            if self.stop_after == "full":
                self.phase_ffn(es)

            cnt = dict(self.cnt)
            with nc.Block() as blk:
                @blk.sync
                def _(e):
                    for n in ("ld", "st", "sA", "sB", "tA", "tB"):
                        if cnt[n]:
                            e.wait_ge(self.sem[n], cnt[n])

                @blk.gpsimd
                def _(e):
                    e.wait_ge(self.sem["gl"], cnt["gl"])
        return nc

    def phase_tables(self, tsb):
        nc = self.nc
        stg_es = contextlib.ExitStack()
        ssb = lambda name, shape, dt: stg_es.enter_context(nc.sbuf_tensor(name, shape, dt))
        TSG = ssb("TSG", [128, 48 * 128], F32)
        TSG2 = ssb("TSG2", [128, 512], F32)
        TMPF = ssb("TMPF", [128, 256], F32)

        def big_view(t_in, n):
            return bass.AP(t_in.tensor, 0, [[128, 128], [128 * 128, n], [1, 128]])

        ld0 = self.cnt["ld"]
        sc0 = self.cnt["sc"]
        with nc.Block() as blk:
            @blk.sync
            def _(e):
                self.sem_inc(e.dma_start(TSG[:, :], big_view(self.tl2_in, 48)), "ld", 16)
                self.sem_inc(e.dma_start(TSG2[:, 0:256], big_view(self.tl1_in, 2)), "ld", 16)
                self.sem_inc(e.dma_start(TSG2[:, 256:512], big_view(self.til1_in, 2)), "ld", 16)
                e.wait_ge(self.sem["ld"], self.cnt["ld"])

            @blk.scalar
            def _(e):
                e.wait_ge(self.sem["ld"], ld0 + 48)
                for t in range(48):
                    self.sem_inc(e.activation(self.TB_L2[:, t * 128:(t + 1) * 128],
                                              TSG[:, t * 128:(t + 1) * 128],
                                              AF.Sin, scale=math.pi / P4), "sc")
                for c in range(2):
                    self.sem_inc(e.activation(self.TB_L1[:, c * 128:(c + 1) * 128],
                                              TSG2[:, c * 128:(c + 1) * 128],
                                              AF.Sin, scale=math.pi / P4), "sc")
                for c in range(2):
                    self.sem_inc(e.activation(TMPF[:, c * 128:(c + 1) * 128],
                                              TSG2[:, 256 + c * 128:256 + (c + 1) * 128],
                                              AF.Sin, scale=math.pi / P4), "sc")

            @blk.vector
            def _(e):
                e.wait_ge(self.sem["sc"], sc0 + 52)
                for c in range(2):
                    self.sem_inc(e.tensor_scalar(
                        self.TB_IL1[:, c * 128:(c + 1) * 128],
                        TMPF[:, c * 128:(c + 1) * 128], 2.0 / P4, None, ALU.mult), "ev")

        with nc.Block() as blk:
            @blk.sync
            def _(e):
                self.sem_inc(e.dma_start(TSG[:, :], big_view(self.til2_in, 48)), "ld", 16)
                e.wait_ge(self.sem["ld"], self.cnt["ld"])

            @blk.scalar
            def _(e):
                e.wait_ge(self.sem["ld"], self.cnt["ld"])
                for t in range(48):
                    self.sem_inc(e.activation(self.TB_IL2[:, t * 128:(t + 1) * 128],
                                              TSG[:, t * 128:(t + 1) * 128],
                                              AF.Sin, scale=math.pi / P4), "sc")

        stg_es.close()

    def conv_fwd_half(self, src_buf, src_col, dh, pre_tensor_waits=(), with_l2=False):
        nc = self.nc
        P1, P2 = self.P1, self.P2

        def evac_waits(e, base_ev, base_sc, idx):
            if idx % 2 == 0:
                e.wait_ge(self.sem["ev"], base_ev + idx // 2 + 1)
            else:
                e.wait_ge(self.sem["sc"], base_sc + idx // 2 + 1)

        pe0, ev0, sc0 = self.cnt["pe"], self.cnt["ev"], self.cnt["sc"]
        gl_leg1_base = self.cnt["gl"]
        with nc.Block() as blk:
            @blk.tensor
            def _(e):
                for n, v in pre_tensor_waits:
                    e.wait_ge(self.sem[n], v)
                for idx in range(32):
                    c, a = idx // 16, idx % 16
                    if idx >= 4:
                        evac_waits(e, ev0, sc0, idx - 4)
                    mm = e.matmul(self.pmm[idx % 4][:, :],
                                  self.TB_L1[:, c * 128:(c + 1) * 128],
                                  src_buf[:, src_col(a):src_col(a) + 512],
                                  start=True, stop=True)
                    self.sem_inc(mm, "pe")

            def evac_body(e, parity, sem_name):
                for idx in range(parity, 32, 2):
                    c, a = idx // 16, idx % 16
                    e.wait_ge(self.sem["pe"], pe0 + idx + 1)
                    dst = P1[:, c * 8192 + a * 512:c * 8192 + (a + 1) * 512]
                    if parity == 0:
                        self.sem_inc(e.tensor_copy(dst, self.pmm[idx % 4][:, :]), sem_name)
                    else:
                        self.sem_inc(e.activation(dst, self.pmm[idx % 4][:, :], AF.Copy),
                                     sem_name)

            @blk.vector
            def _(e):
                evac_body(e, 0, "ev")

            @blk.scalar
            def _(e):
                evac_body(e, 1, "sc")

            @blk.gpsimd
            def _(e):
                for c in range(2):
                    for h in range(2):
                        hi = c * 16 + 8 * h + 8
                        e.wait_ge(self.sem["ev"], ev0 + (hi + 1) // 2)
                        e.wait_ge(self.sem["sc"], sc0 + hi // 2)
                        srcap = bass.AP(P1, c * 8192 + (8 * h) * 512,
                                        [[16384, 128], [512, 8], [1, 512]])
                        dst = bass.AP(self.hs1_t, c * (16 * 128 * DH) + (8 * h) * 65536,
                                      [[512, 128], [65536, 8], [1, 512]])
                        self.sem_inc(e.dma_start(dst, srcap), "gl", 16)
                e.wait_ge(self.sem["gl"], self.cnt["gl"])

            @blk.sync
            def _(e):
                if self.cur_chunk is not None:
                    st_, dh_ = self.cur_chunk
                    q0 = {"sA": self.cnt["sA"], "sB": self.cnt["sB"]}
                    qslot = lambda q: "sA" if q % 2 == 0 else "sB"
                    for q in range(2):
                        for c in range(2):
                            self.sem_inc(e.dma_start(
                                GGBQ(self, q, c),
                                self.gts[st_, c, dh_][:, 4 * q * 512:4 * q * 512 + 2048]),
                                qslot(q), 16)
                    self.gq0 = q0
                for c in range(2):
                    e.wait_ge(self.sem["gl"], gl_leg1_base + 32 * (c + 1))
                    srcap = bass.AP(self.hs1_t, c * (16 * 128 * DH),
                                    [[16 * DH, 128], [1, 16 * DH]])
                    self.sem_inc(e.dma_start(P2[:, c * 8192:(c + 1) * 8192], srcap), "ld", 16)
                self.ld_leg2 = self.cnt["ld"]
                e.wait_ge(self.sem["ld"], self.cnt["ld"])

        if not with_l2:
            return
        pe0, ev0, sc0 = self.cnt["pe"], self.cnt["ev"], self.cnt["sc"]
        with nc.Block() as blk:
            @blk.tensor
            def _(e):
                for idx in range(32):
                    cout, g = idx // 16, idx % 16
                    if idx >= 4:
                        evac_waits(e, ev0, sc0, idx - 4)
                    bank = self.pmm[idx % 4][:, :]
                    if cout == 0:
                        e.matmul(bank, self.TB_L2[:, (0 * 16 + g) * 128:(0 * 16 + g + 1) * 128],
                                 P2[:, 0 * 8192 + g * 512:0 * 8192 + (g + 1) * 512],
                                 start=True, stop=False)
                        mm = e.matmul(bank, self.TB_L2[:, (1 * 16 + g) * 128:(1 * 16 + g + 1) * 128],
                                      P2[:, 1 * 8192 + g * 512:1 * 8192 + (g + 1) * 512],
                                      start=False, stop=True)
                    else:
                        e.matmul(bank, self.TB_L2[:, (2 * 16 + g) * 128:(2 * 16 + g + 1) * 128],
                                 P2[:, 0 * 8192 + g * 512:0 * 8192 + (g + 1) * 512],
                                 start=True, stop=False)
                        mm = e.matmul(bank, self.TB_L2[:, (0 * 16 + g) * 128:(0 * 16 + g + 1) * 128],
                                      P2[:, 1 * 8192 + g * 512:1 * 8192 + (g + 1) * 512],
                                      start=False, stop=True)
                    self.sem_inc(mm, "pe")

            def evac_body2(e, parity, sem_name):
                for idx in range(parity, 32, 2):
                    cout, g = idx // 16, idx % 16
                    e.wait_ge(self.sem["pe"], pe0 + idx + 1)
                    dst = P1[:, cout * 8192 + g * 512:cout * 8192 + (g + 1) * 512]
                    if parity == 0:
                        self.sem_inc(e.tensor_copy(dst, self.pmm[idx % 4][:, :]), sem_name)
                    else:
                        self.sem_inc(e.activation(dst, self.pmm[idx % 4][:, :], AF.Copy),
                                     sem_name)

            @blk.vector
            def _(e):
                evac_body2(e, 0, "ev")

            @blk.scalar
            def _(e):
                evac_body2(e, 1, "sc")

    def conv_mid(self, st, dh):
        nc = self.nc
        P1, P2 = self.P1, self.P2
        s0 = self.gq0
        qslot = lambda q: "sA" if q % 2 == 0 else "sB"
        pe0, ev0, sc0, gp0 = (self.cnt[k] for k in ("pe", "ev", "sc", "gp"))
        gl0, st0 = self.cnt["gl"], self.cnt["st"]
        ld_leg2 = self.ld_leg2

        def l2_evac_wait(e, idx):
            if idx % 2 == 0:
                e.wait_ge(self.sem["ev"], ev0 + idx // 2 + 1)
            else:
                e.wait_ge(self.sem["sc"], sc0 + idx // 2 + 1)

        def inv_evac_wait(e, idx):
            if idx % 2 == 0:
                e.wait_ge(self.sem["ev"], ev0 + 32 + idx // 2 + 1)
            else:
                e.wait_ge(self.sem["sc"], sc0 + 16 + idx // 2 + 1)

        with nc.Block() as blk:
            @blk.tensor
            def _(e):
                e.wait_ge(self.sem["ld"], ld_leg2)
                for idx in range(32):
                    cout, g = idx // 16, idx % 16
                    if idx >= 4:
                        l2_evac_wait(e, idx - 4)
                    bank = self.pmm[idx % 4][:, :]
                    if cout == 0:
                        e.matmul(bank, self.TB_L2[:, (0 * 16 + g) * 128:(0 * 16 + g + 1) * 128],
                                 P2[:, g * 512:(g + 1) * 512], start=True, stop=False)
                        mm = e.matmul(bank, self.TB_L2[:, (1 * 16 + g) * 128:(1 * 16 + g + 1) * 128],
                                      P2[:, 8192 + g * 512:8192 + (g + 1) * 512],
                                      start=False, stop=True)
                    else:
                        e.matmul(bank, self.TB_L2[:, (2 * 16 + g) * 128:(2 * 16 + g + 1) * 128],
                                 P2[:, g * 512:(g + 1) * 512], start=True, stop=False)
                        mm = e.matmul(bank, self.TB_L2[:, (0 * 16 + g) * 128:(0 * 16 + g + 1) * 128],
                                      P2[:, 8192 + g * 512:8192 + (g + 1) * 512],
                                      start=False, stop=True)
                    self.sem_inc(mm, "pe")
                for idx in range(32):
                    cout, g = idx // 16, idx % 16
                    e.wait_ge(self.sem["ev"], ev0 + 16 + g + 1)
                    e.wait_ge(self.sem["gp"], gp0 + g + 1)
                    if idx < 4:
                        l2_evac_wait(e, 28 + idx)
                    else:
                        inv_evac_wait(e, idx - 4)
                    bank = self.pmm[idx % 4][:, :]
                    if cout == 0:
                        e.matmul(bank, self.TB_IL2[:, (0 * 16 + g) * 128:(0 * 16 + g + 1) * 128],
                                 P2[:, g * 512:(g + 1) * 512], start=True, stop=False)
                        mm = e.matmul(bank, self.TB_IL2[:, (2 * 16 + g) * 128:(2 * 16 + g + 1) * 128],
                                      P2[:, 8192 + g * 512:8192 + (g + 1) * 512],
                                      start=False, stop=True)
                    else:
                        e.matmul(bank, self.TB_IL2[:, (1 * 16 + g) * 128:(1 * 16 + g + 1) * 128],
                                 P2[:, g * 512:(g + 1) * 512], start=True, stop=False)
                        mm = e.matmul(bank, self.TB_IL2[:, (0 * 16 + g) * 128:(0 * 16 + g + 1) * 128],
                                      P2[:, 8192 + g * 512:8192 + (g + 1) * 512],
                                      start=False, stop=True)
                    self.sem_inc(mm, "pe")

            def tiles_of(g):
                q = g // 4
                base = (q % 2) * 4096 + (g % 4) * 512
                xre = P1[:, 0 * 8192 + g * 512:0 * 8192 + (g + 1) * 512]
                xim = P1[:, 1 * 8192 + g * 512:1 * 8192 + (g + 1) * 512]
                gre = self.GGB[:, base:base + 512]
                gim = self.GGB[:, 2048 + base:2048 + base + 512]
                zre = P2[:, 0 * 8192 + g * 512:0 * 8192 + (g + 1) * 512]
                zim = P2[:, 1 * 8192 + g * 512:1 * 8192 + (g + 1) * 512]
                return xre, xim, gre, gim, zre, zim

            @blk.vector
            def _(e):
                for idx in range(0, 32, 2):
                    cout, g = idx // 16, idx % 16
                    e.wait_ge(self.sem["pe"], pe0 + idx + 1)
                    self.sem_inc(e.tensor_copy(
                        P1[:, cout * 8192 + g * 512:cout * 8192 + (g + 1) * 512],
                        self.pmm[idx % 4][:, :]), "ev")
                for g in range(16):
                    q = g // 4
                    if g % 4 == 0:
                        e.wait_ge(self.sem[qslot(q)], s0[qslot(q)] + 32 * (q // 2 + 1))
                    e.wait_ge(self.sem["pe"], pe0 + 17 + g)
                    if g % 2 == 1:
                        e.wait_ge(self.sem["sc"], sc0 + (16 + g) // 2 + 1)
                    xre, xim, gre, gim, zre, zim = tiles_of(g)
                    e.tensor_mul(self.PR1[:, :], xre, gre)
                    i1 = e.tensor_mul(self.PR2[:, :], xim, gim)
                    e.wait_ge(self.sem["rwv"], self.sem_inc(i1, "rwv"))
                    self.sem_inc(e.tensor_sub(zre, self.PR1[:, :], self.PR2[:, :]), "ev")
                    e.wait_ge(self.sem["ev"], self.cnt["ev"])
                for idx in range(0, 32, 2):
                    cout, g = idx // 16, idx % 16
                    e.wait_ge(self.sem["pe"], pe0 + 32 + idx + 1)
                    self.sem_inc(e.tensor_copy(
                        P1[:, cout * 8192 + g * 512:cout * 8192 + (g + 1) * 512],
                        self.pmm[idx % 4][:, :]), "ev")

            @blk.scalar
            def _(e):
                for idx in range(1, 32, 2):
                    cout, g = idx // 16, idx % 16
                    e.wait_ge(self.sem["pe"], pe0 + idx + 1)
                    self.sem_inc(e.activation(
                        P1[:, cout * 8192 + g * 512:cout * 8192 + (g + 1) * 512],
                        self.pmm[idx % 4][:, :], AF.Copy), "sc")
                for idx in range(1, 32, 2):
                    cout, g = idx // 16, idx % 16
                    e.wait_ge(self.sem["pe"], pe0 + 32 + idx + 1)
                    self.sem_inc(e.activation(
                        P1[:, cout * 8192 + g * 512:cout * 8192 + (g + 1) * 512],
                        self.pmm[idx % 4][:, :], AF.Copy), "sc")
                e.wait_ge(self.sem["ev"], ev0 + 48)
                for ap_ in range(16):
                    srcap = bass.AP(P1, (ap_ * 8) * 16384 + 1 * 8192,
                                    [[16384, 8], [512, 16], [1, 512]])
                    dst = bass.AP(self.hs2_t, 1 * (128 * 16 * DH) + ap_ * DH,
                                  [[16 * 16 * DH, 8], [16 * DH, 16], [1, DH]])
                    self.sem_inc(e.dma_start(dst, srcap), "st", 16)
                e.wait_ge(self.sem["st"], self.cnt["st"])

            @blk.gpsimd
            def _(e):
                for g in range(16):
                    q = g // 4
                    if g % 4 == 0:
                        e.wait_ge(self.sem[qslot(q)], s0[qslot(q)] + 32 * (q // 2 + 1))
                    e.wait_ge(self.sem["pe"], pe0 + 17 + g)
                    if g % 2 == 0:
                        e.wait_ge(self.sem["ev"], ev0 + (16 + g) // 2 + 1)
                    else:
                        e.wait_ge(self.sem["sc"], sc0 + (16 + g) // 2 + 1)
                    xre, xim, gre, gim, zre, zim = tiles_of(g)
                    e.tensor_mul(self.PR3[:, :], xre, gim)
                    i3 = e.tensor_mul(self.PR4[:, :], xim, gre)
                    e.wait_ge(self.sem["rwg"], self.sem_inc(i3, "rwg"))
                    self.sem_inc(e.tensor_add(zim, self.PR3[:, :], self.PR4[:, :]), "gp")
                    e.wait_ge(self.sem["gp"], self.cnt["gp"])
                e.wait_ge(self.sem["ev"], ev0 + 48)
                e.wait_ge(self.sem["sc"], sc0 + 32)
                for ap_ in range(16):
                    srcap = bass.AP(P1, (ap_ * 8) * 16384,
                                    [[16384, 8], [512, 16], [1, 512]])
                    dst = bass.AP(self.hs2_t, ap_ * DH,
                                  [[16 * 16 * DH, 8], [16 * DH, 16], [1, DH]])
                    self.sem_inc(e.dma_start(dst, srcap), "gl", 16)
                e.wait_ge(self.sem["gl"], self.cnt["gl"])

            @blk.sync
            def _(e):
                for q in (2, 3):
                    e.wait_ge(self.sem["ev"], ev0 + 16 + 4 * (q - 2) + 4)
                    e.wait_ge(self.sem["gp"], gp0 + 4 * (q - 2) + 4)
                    for c in range(2):
                        self.sem_inc(e.dma_start(
                            GGBQ(self, q, c),
                            self.gts[st, c, dh][:, 4 * q * 512:4 * q * 512 + 2048]),
                            qslot(q), 16)
                e.wait_ge(self.sem["gl"], gl0 + 16 * 16)
                e.wait_ge(self.sem["st"], st0 + 16 * 16)
                for c in range(2):
                    srcap = bass.AP(self.hs2_t, c * (128 * 16 * DH),
                                    [[16 * DH, 128], [1, 16 * DH]])
                    self.sem_inc(e.dma_start(P2[:, c * 8192:(c + 1) * 8192], srcap), "ld", 16)
                e.wait_ge(self.sem["ld"], self.cnt["ld"])
                e.wait_ge(self.sem["sA"], self.cnt["sA"])
                e.wait_ge(self.sem["sB"], self.cnt["sB"])

    def conv_inv_half(self, dh, gate_cb):
        nc = self.nc
        P1, P2 = self.P1, self.P2
        self.gate_ev_base = self.cnt["ev"]
        pe0 = self.cnt["pe"]
        with nc.Block() as blk:
            @blk.tensor
            def _(e):
                for ap_ in range(16):
                    if ap_ >= 2:
                        e.wait_ge(self.sem["ev"], self.gate_ev_base + (ap_ - 2) + 1)
                    bank = self.pmm[4 + ap_ % 2][:, :]
                    e.matmul(bank, self.TB_IL1[:, 0:128],
                             P2[:, ap_ * 512:(ap_ + 1) * 512], start=True, stop=False)
                    mm = e.matmul(bank, self.TB_IL1[:, 128:256],
                                  P2[:, 8192 + ap_ * 512:8192 + (ap_ + 1) * 512],
                                  start=False, stop=True)
                    self.sem_inc(mm, "pe")

            @blk.sync
            def _(e):
                self.gate_sync(e)

            @blk.vector
            def _(e):
                for ap_ in range(16):
                    e.wait_ge(self.sem["pe"], pe0 + ap_ + 1)
                    gate_cb(e, ap_, self.pmm[4 + ap_ % 2][:, :])

    def phase_filter_prep(self):
        nc = self.nc
        with contextlib.ExitStack() as pes:
            psb = lambda name, shape, dt: pes.enter_context(nc.sbuf_tensor(name, shape, dt))
            FV = psb("FV", [128, 16 * D], BF16)
            self.P1 = psb("P1f", [128, 16384], BF16)
            self.P2 = psb("P2f", [128, 16384], BF16)

            for st in range(2):
                gl0 = self.cnt["gl"]
                with nc.Block() as blk:
                    @blk.gpsimd
                    def _(e):
                        for a in range(NPH):
                            self.sem_inc(e.dma_start(FV[:, a * D:(a + 1) * D],
                                                     self.ft_in[st, a::16, :]), "gl", 16)
                for dh in range(2):
                    self.cur_chunk = None
                    self.conv_fwd_half(FV, lambda a, dh=dh: a * D + dh * DH, dh,
                                       pre_tensor_waits=[("gl", gl0 + 16 * NPH)],
                                       with_l2=True)
                    with nc.Block() as blk:
                        @blk.sync
                        def _(e):
                            for c in range(2):
                                self.sem_inc(e.dma_start(self.gts[st, c, dh],
                                                         self.P1[:, c * 8192:(c + 1) * 8192]),
                                             "st", 16)
                            e.wait_ge(self.sem["st"], self.cnt["st"])
            del self.P1, self.P2

    def ln_transpose(self, tile_specs, xf_dst):
        nc = self.nc
        n = len(tile_specs)
        pe0, ev0, sc0, te0 = (self.cnt[k] for k in ("pe", "ev", "sc", "te"))
        s0 = {"sA": self.cnt["sA"], "sB": self.cnt["sB"]}
        slot_of = lambda i: "sA" if i % 2 == 0 else "sB"
        slot_w = lambda i: s0[slot_of(i)] + 16 * (i // 2 + 1)
        with nc.Block() as blk:
            @blk.sync
            def _(e):
                for i in range(n):
                    if i >= 2:
                        e.wait_ge(self.sem["ev"], ev0 + 3 * (i - 2) + 3)
                        e.wait_ge(self.sem["sc"], sc0 + 5 * (i - 2) + 1)
                    self.sem_inc(e.dma_start(self.XT[:, (i % 2) * D:(i % 2 + 1) * D],
                                             tile_specs[i][0]), slot_of(i), 16)
                e.wait_ge(self.sem["sA"], self.cnt["sA"])
                e.wait_ge(self.sem["sB"], self.cnt["sB"])

            @blk.tensor
            def _(e):
                e.wait_ge(self.sem["gl"], self.gl_ident)
                for i in range(n):
                    bix = i % 2
                    e.wait_ge(self.sem["ev"], ev0 + 3 * i + 3)
                    if i >= 1:
                        e.wait_ge(self.sem["te"], te0 + i)
                    for k in range(8):
                        mm = e.transpose(self.ptr[:, k * 128:(k + 1) * 128],
                                         self.LNT[:, bix * D + k * 128:bix * D + (k + 1) * 128],
                                         self.IDENT[:, :])
                        if k == 7:
                            self.sem_inc(mm, "pe")

            @blk.vector
            def _(e):
                for i in range(n):
                    bix = i % 2
                    xt = self.XT[:, bix * D:(bix + 1) * D]
                    c = lambda role: self.STT[:, bix * 8 + role:bix * 8 + role + 1]
                    e.wait_ge(self.sem[slot_of(i)], slot_w(i))
                    if i >= 2:
                        e.wait_ge(self.sem["sc"], sc0 + 5 * (i - 2) + 3)
                    self.sem_inc(e.tensor_reduce(c(0), xt, AX.X, ALU.add), "ev")
                    e.wait_ge(self.sem["sc"], sc0 + 5 * i + 3)
                    self.sem_inc(e.scalar_tensor_tensor(c(5), c(1), 1.0 / D, c(3),
                                                        ALU.mult, ALU.subtract), "ev")
                    e.wait_ge(self.sem["sc"], sc0 + 5 * i + 5)
                    if i >= 2:
                        e.wait_ge(self.sem["pe"], pe0 + (i - 2) + 1)
                    self.sem_inc(e.tensor_scalar(self.LNT[:, bix * D:(bix + 1) * D], xt,
                                                 c(2), c(7), ALU.subtract, ALU.mult), "ev")
                    if i >= 1:
                        e.wait_ge(self.sem["pe"], pe0 + i)
                        self.sem_inc(e.tensor_copy(
                            xf_dst[:, tile_specs[i - 1][1]:tile_specs[i - 1][1] + 1024],
                            self.ptr[:, :]), "te")
                e.wait_ge(self.sem["pe"], pe0 + n)
                self.sem_inc(e.tensor_copy(
                    xf_dst[:, tile_specs[n - 1][1]:tile_specs[n - 1][1] + 1024],
                    self.ptr[:, :]), "te")

            @blk.scalar
            def _(e):
                for i in range(n):
                    bix = i % 2
                    xt = self.XT[:, bix * D:(bix + 1) * D]
                    c = lambda role: self.STT[:, bix * 8 + role:bix * 8 + role + 1]
                    e.wait_ge(self.sem[slot_of(i)], slot_w(i))
                    self.sem_inc(e.activation(self.SQ[:, :], xt, AF.Square,
                                              accum_out=c(1)), "sc")
                    e.wait_ge(self.sem["ev"], ev0 + 3 * i + 1)
                    self.sem_inc(e.activation(c(2), c(0), AF.Copy, scale=1.0 / D), "sc")
                    self.sem_inc(e.activation(c(3), c(0), AF.Square, scale=1.0 / D), "sc")
                    e.wait_ge(self.sem["ev"], ev0 + 3 * i + 2)
                    self.sem_inc(e.activation(c(6), c(5), AF.Ln, bias=self.EPS[:, :]), "sc")
                    e.wait_ge(self.sem["sc"], self.cnt["sc"])
                    self.sem_inc(e.activation(c(7), c(6), AF.Exp, scale=-0.5), "sc")

    def phase_batch(self, b, es):
        nc = self.nc
        with contextlib.ExitStack() as bes:
            bsb = lambda name, shape, dt: bes.enter_context(nc.sbuf_tensor(name, shape, dt))
            VA = bsb(f"VA{b}", [128, 16 * D], BF16)

            with contextlib.ExitStack() as ses:
                ssb = lambda name, shape, dt: ses.enter_context(nc.sbuf_tensor(name, shape, dt))
                XF = ssb(f"XF{b}", [128, 16 * D], BF16)
                WW = ssb(f"WW{b}", [128, 8 * 3 * D], BF16)
                PBREP = ssb(f"PBREP{b}", [128, 3 * D], F32)
                GS = ssb(f"GS{b}", [128, 4 * D], BF16)

                gl0 = self.cnt["gl"]
                with nc.Block() as blk:
                    @blk.gpsimd
                    def _(e):
                        for ec in range(8):
                            self.sem_inc(e.dma_start(WW[:, ec * 3072:(ec + 1) * 3072],
                                                     self.pw_in[ec * 128:(ec + 1) * 128, :]),
                                         "gl", 16)

                    @blk.sync
                    def _(e):
                        self.sem_inc(e.dma_start(
                            PBREP[:, :], self.pb_in[None, :].broadcast_to([128, 3 * D])),
                            "ld", 16)

                self.ln_transpose(
                    [(self.x_in[b, a::16, :], a * D) for a in range(NPH)], XF)
                self.maybe_dump(f"xf{b}", XF[:, :], [128, 16 * D], BF16)

                pe0, ev0 = self.cnt["pe"], self.cnt["ev"]
                t0 = {"tA": self.cnt["tA"], "tB": self.cnt["tB"]}
                tslot = lambda a: "tA" if a % 2 == 0 else "tB"
                ld_pb = self.cnt["ld"]
                gl_ww = gl0 + 16 * 8
                with nc.Block() as blk:
                    @blk.tensor
                    def _(e):
                        e.wait_ge(self.sem["gl"], gl_ww)
                        for a in range(NPH):
                            for h in range(2):
                                gidx = a * 2 + h
                                if gidx >= 2:
                                    e.wait_ge(self.sem["ev"], ev0 + 3 * (gidx - 1))
                                for ec in range(8):
                                    for j in range(3):
                                        jj = h * 3 + j
                                        mm = e.matmul(
                                            self.pmm[jj][:, :],
                                            XF[:, a * D + ec * 128:a * D + (ec + 1) * 128],
                                            WW[:, ec * 3072 + jj * 512:ec * 3072 + (jj + 1) * 512],
                                            start=(ec == 0), stop=(ec == 7))
                                self.sem_inc(mm, "pe")

                    @blk.vector
                    def _(e):
                        e.wait_ge(self.sem["ld"], ld_pb)
                        for a in range(NPH):
                            for h in range(2):
                                gidx = a * 2 + h
                                e.wait_ge(self.sem["pe"], pe0 + gidx + 1)
                                if gidx >= 4:
                                    e.wait_ge(self.sem[tslot(a)],
                                              t0[tslot(a)] + 32 * ((a - 2) // 2 + 1))
                                for j in range(3):
                                    jj = h * 3 + j
                                    src = self.pmm[jj][:, :]
                                    brp = PBREP[:, jj * 512:(jj + 1) * 512]
                                    if jj < 2:
                                        dst = VA[:, a * D + jj * 512:a * D + (jj + 1) * 512]
                                    elif jj < 4:
                                        dst = GS[:, (a % 2) * D + (jj - 2) * 512:
                                                 (a % 2) * D + (jj - 1) * 512]
                                    else:
                                        dst = GS[:, 2048 + (a % 2) * D + (jj - 4) * 512:
                                                 2048 + (a % 2) * D + (jj - 3) * 512]
                                    self.sem_inc(e.tensor_tensor(dst, src, brp, ALU.add), "ev")

                    @blk.sync
                    def _(e):
                        for a in range(NPH):
                            e.wait_ge(self.sem["ev"], ev0 + 6 * (a + 1))
                            self.sem_inc(e.dma_start(self.g1sp[b, a],
                                                     GS[:, (a % 2) * D:(a % 2 + 1) * D]),
                                         tslot(a), 16)
                            self.sem_inc(e.dma_start(self.g2sp[b, a],
                                                     GS[:, 2048 + (a % 2) * D:2048 + (a % 2 + 1) * D]),
                                         tslot(a), 16)
                        e.wait_ge(self.sem["tA"], self.cnt["tA"])
                        e.wait_ge(self.sem["tB"], self.cnt["tB"])
                self.maybe_dump(f"va{b}", VA[:, :], [128, 16 * D], BF16)

            if self.batch_stop == "proj":
                return
            with contextlib.ExitStack() as ces:
                csb = lambda name, shape, dt: ces.enter_context(nc.sbuf_tensor(name, shape, dt))
                VB = csb(f"VB{b}", [128, 16 * D], BF16)
                self.P1 = csb(f"P1b{b}", [128, 16384], BF16)
                self.P2 = csb(f"P2b{b}", [128, 16384], BF16)
                self.GGB = csb(f"GGB{b}", [128, 8192], BF16)
                self.PR1 = csb(f"PR1b{b}", [128, 512], BF16)
                self.PR2 = csb(f"PR2b{b}", [128, 512], BF16)
                self.PR3 = csb(f"PR3b{b}", [128, 512], BF16)
                self.PR4 = csb(f"PR4b{b}", [128, 512], BF16)
                GT = csb(f"GT{b}", [128, 2048], BF16)
                TMPG = csb(f"TMPG{b}", [128, 512], F32)
                X1H = csb(f"X1H{b}", [128, 2048], F32)

                nstage = 1 if self.batch_stop == "s1" else 2
                for st in range(nstage):
                    src = VA if st == 0 else VB
                    gsp = self.g1sp if st == 0 else self.g2sp
                    for dh in range(2):
                        self.cur_chunk = (st, dh)
                        self.conv_fwd_half(src, lambda a, dh=dh: a * D + dh * DH, dh)
                        self.conv_mid(st, dh)

                        gate_state = {}

                        gslot = lambda grp: "sA" if grp % 2 == 0 else "sB"
                        gtslot = lambda grp: "tA" if grp % 2 == 0 else "tB"
                        per = 16 if st == 0 else 32
                        g1t = self.g1sp.tensor if hasattr(self.g1sp, "tensor") else None

                        def gate_sync(e, b=b, st=st, dh=dh, gsp=gsp):
                            gate_state["s0"] = {"sA": self.cnt["sA"], "sB": self.cnt["sB"]}
                            gate_state["t0"] = {"tA": self.cnt["tA"], "tB": self.cnt["tB"]}
                            evb = self.gate_ev_base
                            gspt = gsp.tensor
                            xt_t = self.x_in.tensor

                            def store_pair(k):
                                dst = bass.AP(self.x1sp.tensor,
                                              (b * 16 + 2 * k) * 131072 + dh * DH,
                                              [[1024, 128], [131072, 2], [1, 512]])
                                srcap = bass.AP(X1H, (k % 2) * 1024,
                                                [[2048, 128], [512, 2], [1, 512]])
                                self.sem_inc(e.dma_start(dst, srcap), gtslot(k), 16)

                            for grp in range(8):
                                if grp >= 2:
                                    e.wait_ge(self.sem["ev"], evb + 2 * (grp - 2) + 2)
                                    if st == 1:
                                        store_pair(grp - 2)
                                gt_src = bass.AP(gspt,
                                                 (b * 16 + 2 * grp) * 131072 + dh * DH,
                                                 [[1024, 128], [131072, 2], [1, 512]])
                                self.sem_inc(e.dma_start(
                                    GT[:, (grp % 2) * 1024:(grp % 2 + 1) * 1024]
                                    .rearrange("p (a d) -> p a d", a=2), gt_src),
                                    gslot(grp), 16)
                                if st == 1:
                                    x_src = bass.AP(xt_t,
                                                    (b * 2048 + 2 * grp) * 1024 + dh * DH,
                                                    [[16 * 1024, 128], [1024, 2], [1, 512]])
                                    self.sem_inc(e.dma_start(
                                        self.XT[:, (grp % 2) * 1024:(grp % 2 + 1) * 1024]
                                        .rearrange("p (a d) -> p a d", a=2), x_src),
                                        gslot(grp), 16)
                            if st == 1:
                                for k in (6, 7):
                                    e.wait_ge(self.sem["ev"], evb + 2 * k + 2)
                                    store_pair(k)
                                e.wait_ge(self.sem["tA"], self.cnt["tA"])
                                e.wait_ge(self.sem["tB"], self.cnt["tB"])
                            e.wait_ge(self.sem["sA"], self.cnt["sA"])
                            e.wait_ge(self.sem["sB"], self.cnt["sB"])

                        def gt_slice(ap_):
                            grp = ap_ // 2
                            base = (grp % 2) * 1024 + (ap_ % 2) * 512
                            return GT[:, base:base + 512]

                        def xt_slice(ap_):
                            grp = ap_ // 2
                            base = (grp % 2) * 1024 + (ap_ % 2) * 512
                            return self.XT[:, base:base + 512]

                        def x1h_slice(ap_):
                            grp = ap_ // 2
                            base = (grp % 2) * 1024 + (ap_ % 2) * 512
                            return X1H[:, base:base + 512]

                        if st == 0:
                            def gate_cb(e, ap_, psum, dh=dh):
                                grp = ap_ // 2
                                sl = gslot(grp)
                                e.wait_ge(self.sem[sl],
                                          gate_state["s0"][sl] + per * (grp // 2 + 1))
                                self.sem_inc(e.tensor_tensor(
                                    VB[:, ap_ * D + dh * DH:ap_ * D + (dh + 1) * DH],
                                    psum, gt_slice(ap_), ALU.mult), "ev")
                        else:
                            def gate_cb(e, ap_, psum, dh=dh):
                                grp = ap_ // 2
                                sl = gslot(grp)
                                e.wait_ge(self.sem[sl],
                                          gate_state["s0"][sl] + per * (grp // 2 + 1))
                                if grp >= 2:
                                    tl = gtslot(grp)
                                    e.wait_ge(self.sem[tl],
                                              gate_state["t0"][tl] + 16 * ((grp - 2) // 2 + 1))
                                i1 = e.tensor_tensor(TMPG[:, :], psum, gt_slice(ap_),
                                                     ALU.mult)
                                e.wait_ge(self.sem["rwv"], self.sem_inc(i1, "rwv"))
                                self.sem_inc(e.tensor_tensor(
                                    x1h_slice(ap_), TMPG[:, :], xt_slice(ap_),
                                    ALU.add), "ev")
                                e.wait_ge(self.sem["ev"], self.cnt["ev"])

                        self.gate_sync = gate_sync
                        self.conv_inv_half(dh, gate_cb)

                self.maybe_dump(f"vb{b}", VB[:, :], [128, 16 * D], BF16)
                del self.P1, self.P2

            cnt = dict(self.cnt)
            with nc.Block() as blk:
                @blk.sync
                def _(e):
                    for n in ("ld", "st", "sA", "sB", "tA", "tB"):
                        if cnt[n]:
                            e.wait_ge(self.sem[n], cnt[n])

                @blk.gpsimd
                def _(e):
                    e.wait_ge(self.sem["gl"], cnt["gl"])

    def phase_ffn(self, es):
        nc = self.nc
        with contextlib.ExitStack() as fes:
            fsb = lambda name, shape, dt: fes.enter_context(nc.sbuf_tensor(name, shape, dt))
            W1S = fsb("W1S", [128, 8 * 4096], BF16)
            W2S = fsb("W2S", [128, 32 * D], BF16)
            HT = fsb("HT", [128, 32 * 512], BF16)
            LN2FM = fsb("LN2FM", [128, 4 * D], BF16)
            B1C = fsb("B1C", [128, 32], F32)
            B2R = fsb("B2R", [128, D], F32)
            X1T = fsb("X1T", [128, 2 * D], F32)
            OTS = fsb("OTS", [128, 2048], F32)

            gl0 = self.cnt["gl"]
            with nc.Block() as blk:
                @blk.gpsimd
                def _(e):
                    for ec in range(8):
                        self.sem_inc(e.dma_start(W1S[:, ec * 4096:(ec + 1) * 4096],
                                                 self.w1_in[ec * 128:(ec + 1) * 128, :]),
                                     "gl", 16)
                    for hc in range(32):
                        self.sem_inc(e.dma_start(W2S[:, hc * D:(hc + 1) * D],
                                                 self.w2_in[hc * 128:(hc + 1) * 128, :]),
                                     "gl", 16)

                @blk.sync
                def _(e):
                    self.sem_inc(e.dma_start(
                        B1C[:, :], self.b1_in.rearrange("(c p) -> p c", p=128),
                        allow_slow_non_contiguous=True), "ld", 16)
                    self.sem_inc(e.dma_start(
                        B2R[:, :], self.b2_in[None, :].broadcast_to([128, D])), "ld", 16)
                    e.wait_ge(self.sem["ld"], self.cnt["ld"])
            gl_w = self.cnt["gl"]

            if self.ffn_stop == "load":
                return
            for b in range(NB):
                for ch in range(4):
                    tiles = [4 * ch + i for i in range(4)]
                    self.ln_transpose(
                        [(self.x1sp[b, t], i * D) for i, t in enumerate(tiles)], LN2FM)
                    if self.ffn_stop == "ln":
                        continue

                    pe0, sc0 = self.cnt["pe"], self.cnt["sc"]
                    with nc.Block() as blk:
                        @blk.tensor
                        def _(e):
                            e.wait_ge(self.sem["gl"], gl_w)
                            for hc in range(32):
                                if hc >= 2:
                                    e.wait_ge(self.sem["sc"], sc0 + hc - 1)
                                for ec in range(8):
                                    rhs = bass.AP(LN2FM, ec * 128,
                                                  [[4 * D, 128], [D, 4], [1, 128]])
                                    mm = e.matmul(self.pmm[hc % 2][:, :],
                                                  W1S[:, ec * 4096 + hc * 128:
                                                      ec * 4096 + (hc + 1) * 128],
                                                  rhs, start=(ec == 0), stop=(ec == 7))
                                self.sem_inc(mm, "pe")

                        @blk.scalar
                        def _(e):
                            for hc in range(32):
                                e.wait_ge(self.sem["pe"], pe0 + hc + 1)
                                self.sem_inc(e.activation(
                                    HT[:, hc * 512:(hc + 1) * 512],
                                    self.pmm[hc % 2][:, :], AF.Gelu,
                                    bias=B1C[:, hc:hc + 1]), "sc")

                    if self.ffn_stop == "ffn1":
                        continue
                    if self.ffn_stop == "ffn2first" and (b, ch) != (0, 0):
                        continue
                    pe0, ev0 = (self.cnt[k] for k in ("pe", "ev"))
                    s0f = {"sA": self.cnt["sA"], "sB": self.cnt["sB"]}
                    t0f = {"tA": self.cnt["tA"], "tB": self.cnt["tB"]}
                    fslot = lambda i: "sA" if i % 2 == 0 else "sB"
                    ftslot = lambda gi: "tA" if gi % 2 == 0 else "tB"
                    with nc.Block() as blk:
                        @blk.sync
                        def _(e):
                            for i in (0, 1):
                                self.sem_inc(e.dma_start(
                                    self.X1T_dst(X1T, i), self.x1sp[b, tiles[i]]),
                                    fslot(i), 16)
                            for gi in range(8):
                                i, j = gi // 2, gi % 2
                                e.wait_ge(self.sem["ev"], ev0 + gi + 1)
                                slot = (gi % 2) * 1024 + 512
                                self.sem_inc(e.dma_start(
                                    self.out[b, tiles[i]::16, j * 512:(j + 1) * 512],
                                    OTS[:, slot:slot + 512]), ftslot(gi), 16)
                                if gi in (1, 3):
                                    ii = 2 + (gi - 1) // 2
                                    self.sem_inc(e.dma_start(
                                        self.X1T_dst(X1T, ii), self.x1sp[b, tiles[ii]]),
                                        fslot(ii), 16)
                            e.wait_ge(self.sem["tA"], self.cnt["tA"])
                            e.wait_ge(self.sem["tB"], self.cnt["tB"])
                            e.wait_ge(self.sem["sA"], self.cnt["sA"])
                            e.wait_ge(self.sem["sB"], self.cnt["sB"])

                        @blk.tensor
                        def _(e):
                            for i in range(4):
                                if i >= 2:
                                    e.wait_ge(self.sem["ev"], ev0 + 2 * (i - 2) + 2)
                                b0 = self.pmm[2 + 2 * (i % 2)][:, :]
                                b1 = self.pmm[3 + 2 * (i % 2)][:, :]
                                for hc in range(32):
                                    lhsT = HT[:, hc * 512 + i * 128:hc * 512 + (i + 1) * 128]
                                    e.matmul(b0, lhsT, W2S[:, hc * D:hc * D + 512],
                                             start=(hc == 0), stop=(hc == 31))
                                    mm = e.matmul(b1, lhsT, W2S[:, hc * D + 512:(hc + 1) * D],
                                                  start=(hc == 0), stop=(hc == 31))
                                self.sem_inc(mm, "pe")

                        @blk.vector
                        def _(e):
                            for gi in range(8):
                                i, j = gi // 2, gi % 2
                                e.wait_ge(self.sem["pe"], pe0 + i + 1)
                                e.wait_ge(self.sem[fslot(i)], s0f[fslot(i)] + 16 * (i // 2 + 1))
                                if gi >= 2:
                                    tl = ftslot(gi)
                                    e.wait_ge(self.sem[tl], t0f[tl] + 16 * ((gi - 2) // 2 + 1))
                                slot = (gi % 2) * 1024
                                i1 = e.tensor_tensor(OTS[:, slot:slot + 512],
                                                     self.pmm[2 + 2 * (i % 2) + j][:, :],
                                                     X1T[:, (i % 2) * D + j * 512:
                                                         (i % 2) * D + (j + 1) * 512],
                                                     ALU.add)
                                e.wait_ge(self.sem["rwv"], self.sem_inc(i1, "rwv"))
                                self.sem_inc(e.tensor_tensor(
                                    OTS[:, slot + 512:slot + 1024],
                                    OTS[:, slot:slot + 512],
                                    B2R[:, j * 512:(j + 1) * 512], ALU.add), "ev")

    def X1T_dst(self, X1T, i):
        return X1T[:, (i % 2) * D:(i % 2 + 1) * D]


_CACHE = {}


def _build():
    if "nc" not in _CACHE:
        b = Builder()
        _CACHE["nc"] = b.build()
    return _CACHE["nc"]


def kernel(L0_attn_x, L0_attn_proj_w, L0_attn_proj_b, L0_attn_filter_time,
           L0_ffn_w1, L0_ffn_b1, L0_ffn_w2, L0_ffn_b2):
    nc = _build()
    t_l1, t_l2, t_il2, t_il1 = host_tables()
    ident = np.eye(128, dtype=np.float32)
    x = np.ascontiguousarray(np.asarray(L0_attn_x, dtype=np.float32))
    ft = np.ascontiguousarray(np.asarray(L0_attn_filter_time, dtype=np.float32)
                              .reshape(2, L, D))
    shared = {
        "pw": np.ascontiguousarray(L0_attn_proj_w, dtype=np.float32),
        "pb": np.ascontiguousarray(L0_attn_proj_b, dtype=np.float32),
        "ft": ft,
        "w1": np.ascontiguousarray(L0_ffn_w1, dtype=np.float32),
        "b1": np.ascontiguousarray(L0_ffn_b1, dtype=np.float32),
        "w2": np.ascontiguousarray(L0_ffn_w2, dtype=np.float32),
        "b2": np.ascontiguousarray(L0_ffn_b2, dtype=np.float32),
        "tl1": t_l1, "tl2": t_l2, "til2": t_il2, "til1": t_il1,
        "idm": ident,
    }
    in_maps = []
    for c in range(NCORES):
        m = dict(shared)
        m["x"] = np.ascontiguousarray(x[c * NB:(c + 1) * NB])
        in_maps.append(m)
    res = run_bass_kernel_spmd(nc, in_maps, core_ids=list(range(NCORES)))
    return np.concatenate([r["out"] for r in res.results], axis=0)
